# version 02
# baseline (speedup 1.0000x reference)
import os
from collections import defaultdict
from contextlib import ExitStack

import numpy as np

import concourse.bass as bass
import concourse.mybir as mybir
from concourse.bass_utils import run_bass_kernel_spmd

F32 = mybir.dt.float32
BF16 = mybir.dt.bfloat16
I8 = mybir.dt.int8
AF = mybir.ActivationFunctionType
ALU = mybir.AluOpType

B, S, D, H, HD, LAT, PD = 2, 4096, 2048, 16, 128, 512, 64
FF, LR = 8192, 16
TQ = 1024
FC = D // 128
NCORE = 8

_MAX_IMPL = 5
_STAGE = min(int(os.environ.get("MK_STAGE", "99")), _MAX_IMPL)
_NH = int(os.environ.get("MK_NH", "16"))
_PAR = int(os.environ.get("MK_PARANOID", "3"))
_BCUT = int(os.environ.get("MK_BCUT", "99"))
_PAR_A = bool(_PAR & 1)
_PAR_B = bool(_PAR & 2)
_PAR_C = bool(_PAR & 4)
_DBG = set(s for s in os.environ.get("MK_DBG", "").split(",") if s)


class Ctx:

    ENGINES = ("tensor", "vector", "scalar", "gpsimd", "sync")

    def __init__(self, nc, es):
        self.nc = nc
        self.es = es
        self.q = {e: [] for e in self.ENGINES}
        self.cnt = defaultdict(int)
        self.pools = {"d": [], "e": []}
        self.nalloc = {"d": 0, "e": 0}
        self.slot_of = {}
        self.waits = []

    POOL_CAP = {"d": 22, "e": 79}

    SETUP_CHAIN = {"b_neg", "b_nbc", "b_nsb", "b_bdr", "b_bde", "b_bdb",
                   "b_bdp", "b_pcr", "b_pce", "b_pcm", "b_ccb", "b_cc"}
    MERGE = {
        "b_osq": "b_rms", "b_rmm": "b_rms", "b_rc": "b_rms", "b_rs": "b_rms",
        "b_rrec": "b_rms", "b_rr": "b_rms",
        "b_ktp": "b_kv", "b_kvmm": "b_kv", "b_kvup": "b_kv", "b_kvbf": "b_kv",
        "b_corb": "b_cor", "b_cor_mm": "b_cor", "b_cora": "b_cor",
        "b_scmm": "b_sc", "b_prb": "b_sc",
        "b_omm": "b_oe", "b_oev": "b_oe",
        "b_qde": "b_qd", "b_qdb": "b_qd",
    }

    @staticmethod
    def _c(name):
        if name.startswith("f1_") or name.startswith("f2_"):
            return "f_" + name[3:]
        if name.startswith("bln0") or name.startswith("bln1"):
            name = "bln" + name[4:]
        if name in Ctx.SETUP_CHAIN:
            return "b_set"
        return Ctx.MERGE.get(name, name)

    def _slot(self, name, klass):
        name = self._c(name)
        if name in self.slot_of:
            sid, kl = self.slot_of[name]
            assert klass is None or kl == klass, f"sem {name} class {kl}->{klass}"
            return sid
        assert klass is not None, f"first touch of sem {name} without class"
        pool = self.pools[klass]
        idx = self.nalloc[klass]
        self.nalloc[klass] += 1
        assert idx < self.POOL_CAP[klass], f"semaphore pool {klass} exhausted at {name}"
        pool.append(self.es.enter_context(
            self.nc.semaphore(f"{klass}{len(pool)}")))
        sid = (klass, idx)
        self.slot_of[name] = (sid, klass)
        return sid

    def sem(self, name, klass=None):
        sid = self._slot(name, klass)
        return self.pools[sid[0]][sid[1]]

    def inc(self, name, n):
        sid = self._slot(name, "d" if n >= 16 else "e")
        self.cnt[sid] += n
        return self.cnt[sid]

    def cur(self, name):
        name = self._c(name)
        if name not in self.slot_of:
            return 0
        return self.cnt[self.slot_of[name][0]]

    def emit(self, eng, fn):
        self.q[eng].append(fn)

    def dma(self, eng, dst, src, sem):
        tgt = self.inc(sem, 16)
        s = self.sem(sem)
        self.emit(eng, lambda e, d=dst, i=src, s=s: e.dma_start(d, i).then_inc(s, 16))
        return tgt

    def wait(self, eng, sem, val=None):
        sem = self._c(sem)
        if val is None:
            val = self.cur(sem)
        if val <= 0:
            return
        s = self.sem(sem)
        self.waits.append((sem, val))
        self.emit(eng, lambda e, s=s, v=val: e.wait_ge(s, v))

    def op(self, eng, fn, sem=None, n=1):
        if sem is None:
            self.emit(eng, fn)
            return None
        tgt = self.inc(sem, n)
        s = self.sem(sem)
        self.emit(eng, lambda e, f=fn, s=s, n=n: f(e).then_inc(s, n))
        return tgt

    def check(self):
        for sem, val in self.waits:
            assert val <= self.cur(sem), (
                f"wait on {sem} for {val} but total emitted is {self.cur(sem)} (would hang)"
            )

    def finalize(self):
        self.check()
        with self.nc.Block() as blk:
            @blk.tensor
            def _(t):
                for f in self.q["tensor"]:
                    f(t)

            @blk.vector
            def _(v):
                for f in self.q["vector"]:
                    f(v)

            @blk.scalar
            def _(s):
                for f in self.q["scalar"]:
                    f(s)

            @blk.gpsimd
            def _(g):
                for f in self.q["gpsimd"]:
                    f(g)

            @blk.sync
            def _(sy):
                for f in self.q["sync"]:
                    f(sy)



def _host_prepare(inp):
    f32 = lambda a: np.ascontiguousarray(np.asarray(a, dtype=np.float32))
    x0 = f32(inp["L0_attn_a_hidden_states"])

    common = {}
    common["wdkv"] = f32(inp["L0_attn_a_W_DKV"])
    common["wdq"] = f32(inp["L0_attn_a_W_DQ"])
    common["wuk"] = f32(inp["L0_attn_a_W_UK"])
    common["wuv"] = f32(inp["L0_attn_a_W_UV"])
    common["wuq"] = f32(inp["L0_attn_a_W_UQ"])
    common["wkr"] = f32(inp["L0_attn_a_W_KR"])
    common["wqr"] = f32(inp["L0_attn_a_W_QR"])
    common["wo"] = f32(inp["L0_attn_a_W_O"])
    inv = (1.0 / 10000.0 ** (np.arange(0, PD, 2, dtype=np.float64) / PD))
    t = np.arange(S, dtype=np.float64)
    fr = t[:, None] * inv[None, :]
    emb = np.concatenate([fr, fr], axis=1)
    common["rcos"] = np.ascontiguousarray(np.cos(emb).T.astype(np.float32))
    sgn = np.concatenate([-np.ones(32), np.ones(32)])[:, None]
    common["rsin"] = np.ascontiguousarray((np.sin(emb).T * sgn).astype(np.float32))

    wupT = np.asarray(inp["L0_ffn_w_up_q"]).T.astype(np.int8)
    common["wupT"] = np.ascontiguousarray(
        wupT.reshape(16, 128, 32, 256).transpose(2, 1, 0, 3))
    common["supT"] = np.ascontiguousarray(f32(inp["L0_ffn_w_up_scale"]).T)
    common["bup"] = np.ascontiguousarray(f32(inp["L0_ffn_b_up"]).reshape(FF // 128, 128).T)
    common["aup"] = f32(inp["L0_ffn_w_up_lora_a"])
    common["bupl"] = f32(inp["L0_ffn_w_up_lora_b"])
    wdnT = np.asarray(inp["L0_ffn_w_down_q"]).T.astype(np.int8)
    common["wdnT"] = np.ascontiguousarray(
        wdnT.reshape(64, 128, 16, 128).transpose(2, 1, 0, 3))
    common["sdnT"] = np.ascontiguousarray(f32(inp["L0_ffn_w_down_scale"]).T)
    common["bdn"] = np.ascontiguousarray(f32(inp["L0_ffn_b_down"]).reshape(FC, 128).T)
    common["adn"] = f32(inp["L0_ffn_w_down_lora_a"])
    common["bdnl"] = f32(inp["L0_ffn_w_down_lora_b"])

    common["slope"] = np.ascontiguousarray(f32(inp["L0_attn_b_slope_rate"]).reshape(1, H))
    common["wqkvT"] = np.ascontiguousarray(f32(inp["L0_attn_b_w_qkv"]).T)
    common["wgateT"] = np.ascontiguousarray(f32(inp["L0_attn_b_w_gate"]).T)
    common["woutT"] = np.ascontiguousarray(f32(inp["L0_attn_b_w_out"]).T)
    common["normw"] = np.ascontiguousarray(f32(inp["L0_attn_b_norm_w"]).reshape(FC, 128).T)
    common["qposg"] = np.arange(1, TQ + 1, dtype=np.float32).reshape(1, TQ)
    common["qposl"] = ((np.arange(TQ) % 256) + 1).astype(np.float32).reshape(1, TQ)
    common["kdecg"] = np.ascontiguousarray(
        (255 - np.arange(256)).astype(np.float32).reshape(2, 128).T)
    n_ = np.arange(256)
    m_ = np.arange(256)
    common["dgexp"] = np.ascontiguousarray(
        np.maximum(m_[None, :] - n_[:, None], 0).astype(np.float32).reshape(2, 128, 256))
    common["dgmask"] = np.ascontiguousarray(
        (m_[None, :] >= n_[:, None]).astype(np.float32).reshape(2, 128, 256))
    common["ident"] = np.eye(128, dtype=np.float32)

    in_maps = []
    for c in range(NCORE):
        b, q = c // 4, c % 4
        m = dict(common)
        x0t = np.ascontiguousarray(x0[b].T)
        m["x0f"] = x0t
        m["x0own"] = np.ascontiguousarray(x0t[:, q * TQ:(q + 1) * TQ])
        m["rcown"] = np.ascontiguousarray(common["rcos"][:, q * TQ:(q + 1) * TQ])
        m["rsown"] = np.ascontiguousarray(common["rsin"][:, q * TQ:(q + 1) * TQ])
        pm = np.zeros((128, 4), dtype=np.float32)
        pm[:, q] = 1.0
        m["pmask"] = pm
        pg = np.zeros((1, 4), dtype=np.float32)
        pc = np.zeros((1, 4), dtype=np.float32)
        for p in range(4):
            if p < q:
                pg[0, p] = 1024.0 * (q - p - 1)
                pc[0, p] = 1.0
        m["pcoefg"] = pg
        m["pcmask"] = pc
        in_maps.append(m)
    return in_maps



def _build():
    nc = bass.Bass("TRN2", target_bir_lowering=False, debug=False, num_devices=8)
    es = ExitStack()
    ctx = Ctx(nc, es)

    dr = {}
    def din(name, shape, dtype=F32):
        dr[name] = nc.dram_tensor(name, list(shape), dtype, kind="ExternalInput")

    din("x0f", (D, S)); din("x0own", (D, TQ))
    for n, sh in [("wdkv", (D, LAT)), ("wdq", (D, LAT)), ("wuk", (LAT, D)),
                  ("wuv", (LAT, D)), ("wuq", (LAT, D)), ("wkr", (D, PD)),
                  ("wqr", (LAT, H * PD)), ("wo", (D, D)),
                  ("rcos", (PD, S)), ("rsin", (PD, S))]:
        din(n, sh)
    din("wupT", (32, 128, 16, 256), I8); din("supT", (32, FF)); din("bup", (128, FF // 128))
    din("aup", (D, LR)); din("bupl", (LR, FF))
    din("wdnT", (16, 128, 64, 128), I8); din("sdnT", (128, D)); din("bdn", (128, FC))
    din("adn", (FF, LR)); din("bdnl", (LR, D))
    din("slope", (1, H)); din("wqkvT", (D, 3 * H * HD)); din("wgateT", (D, D))
    din("woutT", (D, D)); din("normw", (128, FC))
    din("qposg", (1, TQ)); din("qposl", (1, TQ)); din("kdecg", (128, 2))
    din("dgexp", (2, 128, 256)); din("dgmask", (2, 128, 256))
    din("ident", (128, 128)); din("pmask", (128, 4)); din("pcoefg", (1, 4))
    din("pcmask", (1, 4)); din("rcown", (PD, TQ)); din("rsown", (PD, TQ))

    yf = nc.dram_tensor("yf", [D, TQ], F32, kind="ExternalOutput")

    dbg = {}
    def dout(name, shape, dtype=F32):
        if name in _DBG:
            dbg[name] = nc.dram_tensor("dbg_" + name, list(shape), dtype,
                                       kind="ExternalOutput")
        return dbg.get(name)

    sb = {}
    def salloc(stack, name, shape, dtype):
        sb[name] = stack.enter_context(nc.sbuf_tensor(name, list(shape), dtype))
        return sb[name]

    salloc(es, "resid", [128, FC, TQ], F32)
    salloc(es, "onesb", [128, 1], BF16)
    salloc(es, "onesf", [128, 1], F32)
    salloc(es, "ones1f", [1, 128], F32)
    salloc(es, "identt", [128, 128], BF16)
    salloc(es, "nrow", [1, 16], F32)
    salloc(es, "brow", [1, 16], F32)
    salloc(es, "drow", [1, 16], F32)

    ctx.op("gpsimd", lambda e: e.memset(sb["onesb"][:, :], 1.0))
    ctx.op("gpsimd", lambda e: e.memset(sb["onesf"][:, :], 1.0))
    ctx.op("gpsimd", lambda e: e.memset(sb["ones1f"][:, :], 1.0), sem="const")
    ctx.dma("gpsimd", sb["identt"][:, :], dr["ident"].ap(), "const_ld")
    ctx.dma("sync", sb["resid"][:, :, :],
            dr["x0own"].ap().rearrange("(c p) t -> p c t", p=128), "resid_ld")
    ctx.inc("resid_ok", 0)

    if _STAGE >= 1:
        _p_attn_a(ctx, nc, es, dr, sb, dout, salloc)
    if _STAGE >= 3:
        _p_ffn(ctx, nc, es, dr, sb, dout, salloc, tag="f1")
    if _STAGE >= 4:
        _p_attn_b(ctx, nc, es, dr, sb, dout, salloc)
    if _STAGE >= 5:
        _p_ffn(ctx, nc, es, dr, sb, dout, salloc, tag="f2")

    ctx.wait("sync", "resid_ld")
    ctx.wait("sync", "resid_ok")
    ctx.dma("sync", yf.ap().rearrange("(c p) t -> p c t", p=128), sb["resid"][:, :, :],
            "store")
    ctx.wait("sync", "store")
    ctx.wait("sync", "dbg")

    ctx.finalize()
    es.close()
    return nc



def _p_attn_a(ctx, nc, es, dr, sb, dout, salloc):
    SL = 256
    NSL = S // SL

    aes = ExitStack()
    salloc(aes, "ckv", [128, 4, S], BF16)
    salloc(aes, "krr", [64, S], BF16)
    salloc(aes, "cq", [128, 4, TQ], BF16)

    ckv, krr, cq, resid = sb["ckv"], sb["krr"], sb["cq"], sb["resid"]

    p1 = ExitStack()
    xsl = [p1.enter_context(nc.sbuf_tensor(f"xsl{i}", [128, FC, SL], F32)) for i in range(2)]
    xb = p1.enter_context(nc.sbuf_tensor("xb", [128, FC, SL], BF16))
    hsl = [p1.enter_context(nc.sbuf_tensor(f"hsl{i}", [128, FC, SL], BF16)) for i in range(2)]
    wproj = p1.enter_context(nc.sbuf_tensor("wproj", [128, FC, LAT], BF16))
    wkr = p1.enter_context(nc.sbuf_tensor("wkrS", [128, FC, PD], BF16))
    strow = p1.enter_context(nc.sbuf_tensor("strow", [1, 2, 4, SL], F32))

    pps = ExitStack()
    ckv_ps = [pps.enter_context(nc.psum_tensor(f"ckvps{i}", [128, 4, SL], F32))
              for i in range(2)]
    st_ps = [pps.enter_context(nc.psum_tensor(f"stps{i}", [1, 2, SL], F32)) for i in range(2)]
    kr_ps = pps.enter_context(nc.psum_tensor("krps", [64, SL], F32))
    bc_ps = pps.enter_context(nc.psum_tensor("bcps", [128, 2, SL], F32))

    ctx.dma("gpsimd", wproj[:, :, :], dr["wdkv"].ap().rearrange("(c p) l -> p c l", p=128),
            "aw")
    ctx.dma("gpsimd", wkr[:, :, :], dr["wkr"].ap().rearrange("(c p) l -> p c l", p=128),
            "aw")
    w_ld = ctx.cur("aw")

    def sweep_slice(j, src_is_resid, proj_dst, proj_ps_sem, proj_evac_sem, do_kr):
        bi = j % 2
        tsl = slice((j % NSL) * SL, ((j % NSL) + 1) * SL) if not src_is_resid else \
            slice((j - NSL) * SL, (j - NSL + 1) * SL)

        if not src_is_resid:
            ctx.wait("sync", "a_h", max(0, j - 1))
            ctx.dma("sync", xsl[bi][:, :, :],
                    dr["x0f"].ap().rearrange("(c p) t -> p c t", p=128)[:, :, tsl], "a_x")
        ctx.wait("scalar", "a_stats_mm", max(0, j))
        if not src_is_resid:
            ctx.wait("scalar", "a_x", 16 * (j + 1))
        else:
            ctx.wait("scalar", "resid_ld")
        src = xsl[bi][:, :, :] if not src_is_resid else resid[:, :, tsl]
        ctx.op("scalar", lambda e, s=src: e.activation(xb[:, :, :], s, AF.Copy),
               sem="a_cast")
        ctx.wait("vector", "a_cast", j + 1)
        ctx.wait("vector", "a_proj_mm", max(0, j - 1))
        ctx.op("vector", lambda e, b=bi:
               e.tensor_mul(hsl[b][:, :, :], xb[:, :, :], xb[:, :, :]), sem="a_sq")
        ctx.wait("tensor", "a_sq", j + 1)
        ctx.wait("tensor", "a_statcalc", max(0, j - 1))
        def stats_mm(e, b=bi):
            for kc in range(FC):
                e.matmul(st_ps[b][:, 0, :], sb["onesb"][:, :], xb[:, kc, :],
                         start=(kc == 0), stop=(kc == FC - 1))
            for kc in range(FC):
                r = e.matmul(st_ps[b][:, 1, :], sb["onesb"][:, :], hsl[b][:, kc, :],
                             start=(kc == 0), stop=(kc == FC - 1))
            return r
        ctx.op("tensor", stats_mm, sem="a_stats_mm")
        ctx.wait("vector", "a_stats_mm", j + 1)
        def statcalc(e, b=bi):
            e.tensor_scalar_mul(strow[:, b, 0, :], st_ps[b][:, 0, :], 1.0 / D)
            e.tensor_scalar_mul(strow[:, b, 2, :], st_ps[b][:, 1, :], 1.0 / D)
            e.tensor_mul(strow[:, b, 3, :], strow[:, b, 0, :], strow[:, b, 0, :])
            e.tensor_sub(strow[:, b, 2, :], strow[:, b, 2, :], strow[:, b, 3, :])
            return e.tensor_scalar_add(strow[:, b, 2, :], strow[:, b, 2, :], 1e-5)
        ctx.op("vector", statcalc, sem="a_statcalc")
        ctx.wait("scalar", "a_statcalc", j + 1)
        ctx.op("scalar", lambda e, b=bi:
               e.activation(strow[:, b, 2, :], strow[:, b, 2, :], AF.Sqrt),
               sem="a_sd")
        ctx.wait("vector", "a_sd", j + 1)
        ctx.op("vector", lambda e, b=bi:
               e.reciprocal(strow[:, b, 1, :], strow[:, b, 2, :]), sem="a_rinv")
        ctx.wait("tensor", "a_rinv", j + 1)
        ctx.wait("tensor", "a_h", max(0, j))
        def bcast(e, b=bi):
            e.matmul(bc_ps[:, 0, :], sb["ones1f"][:, :], strow[:, b, 0, :],
                     start=True, stop=True)
            return e.matmul(bc_ps[:, 1, :], sb["ones1f"][:, :], strow[:, b, 1, :],
                            start=True, stop=True)
        ctx.op("tensor", bcast, sem="a_bc")
        ctx.wait("vector", "a_bc", j + 1)
        def happly(e, b=bi, s=src):
            mb = bc_ps[:, 0:1, :].broadcast_to([128, FC, SL])
            rb = bc_ps[:, 1:2, :].broadcast_to([128, FC, SL])
            e.tensor_sub(xsl[b][:, :, :], s, mb)
            return e.tensor_mul(hsl[b][:, :, :], xsl[b][:, :, :], rb)
        ctx.op("vector", happly, sem="a_h")
        ctx.wait("tensor", "a_h", j + 1)
        if j == 0:
            ctx.wait("tensor", "aw", w_ld)
        if j == NSL:
            ctx.wait("tensor", "aw", ctx.cur("aw"))
        ctx.wait("tensor", proj_evac_sem, max(0, ctx.cur(proj_ps_sem) - 1))
        if do_kr:
            ctx.wait("tensor", "a_kr_evac", ctx.cur("a_kr_evac"))
        def proj_mm(e, b=bi):
            for lc in range(4):
                for kc in range(FC):
                    r = e.matmul(ckv_ps[b][:, lc, :],
                                 wproj[:, kc, lc * 128:(lc + 1) * 128], hsl[b][:, kc, :],
                                 start=(kc == 0), stop=(kc == FC - 1))
            if do_kr:
                for kc in range(FC):
                    r = e.matmul(kr_ps[:, :], wkr[:, kc, :], hsl[b][:, kc, :],
                                 start=(kc == 0), stop=(kc == FC - 1))
            return r
        ctx.op("tensor", proj_mm, sem=proj_ps_sem)
        ctx.op("tensor", lambda e: e.nop(), sem="a_proj_mm")
        ctx.wait("scalar", proj_ps_sem, ctx.cur(proj_ps_sem))
        if do_kr:
            ctx.op("scalar", lambda e, t=tsl:
                   e.activation(krr[0:64, t], kr_ps[:, :], AF.Copy), sem="a_kr_evac")
        ctx.op("scalar", lambda e, b=bi, t=tsl, d=proj_dst:
               e.activation(d[:, :, t], ckv_ps[b][:, :, :], AF.Copy), sem=proj_evac_sem)

    for j in range(NSL):
        sweep_slice(j, False, ckv, "a_ckv_mm", "a_ckv_evac", True)

    ctx.wait("gpsimd", "a_proj_mm", NSL)
    ctx.dma("gpsimd", wproj[:, :, :], dr["wdq"].ap().rearrange("(c p) l -> p c l", p=128),
            "aw")

    ctx.wait("tensor", "a_ckv_evac", ctx.cur("a_ckv_evac"))
    ctx.wait("tensor", "a_kr_evac", ctx.cur("a_kr_evac"))
    for j in range(NSL, NSL + TQ // SL):
        sweep_slice(j, True, cq, "a_cq_mm", "a_cq_evac", False)

    n_cq = ctx.cur("a_cq_evac")
    n_ckv = ctx.cur("a_ckv_evac")
    n_kr = ctx.cur("a_kr_evac")

    ctx.wait("sync", "a_ckv_evac", n_ckv)
    ctx.wait("sync", "a_cq_evac", n_cq)
    pps.close()
    p1.close()

    p2 = ExitStack()
    rc = p2.enter_context(nc.sbuf_tensor("rcS", [64, S], F32))
    rs = p2.enter_context(nc.sbuf_tensor("rsS", [64, S], F32))
    krtmp = p2.enter_context(nc.sbuf_tensor("krtmp", [64, S], BF16))
    ctx.dma("sync", rc[:, :], dr["rcos"].ap(), "a_rope_ld")
    ctx.dma("sync", rs[:, :], dr["rsin"].ap(), "a_rope_ld")
    ctx.wait("sync", "a_kr_evac", n_kr)
    ctx.dma("sync", krtmp[0:32, :], krr[32:64, :], "a_rope_ld")
    ctx.dma("sync", krtmp[32:64, :], krr[0:32, :], "a_rope_ld")
    ctx.wait("vector", "a_rope_ld")
    ctx.wait("vector", "a_kr_evac", n_kr)
    def rope_kr(e):
        e.tensor_mul(krr[0:64, :], krr[0:64, :], rc[0:64, :])
        e.tensor_mul(krtmp[0:64, :], krtmp[0:64, :], rs[0:64, :])
        return e.tensor_add(krr[0:64, :], krr[0:64, :], krtmp[0:64, :])
    ctx.op("vector", rope_kr, sem="a_rope")
    ctx.wait("sync", "a_rope", 1)
    p2.close()

    if "ckv" in _DBG:
        t = dout("ckv", (128, 4, S), BF16)
        ctx.wait("sync", "a_ckv_evac", n_ckv)
        ctx.dma("sync", t.ap(), ckv[:, :, :], "dbg")
    if "krr" in _DBG:
        t = dout("krr", (64, S), BF16)
        ctx.wait("sync", "a_rope", 1)
        ctx.dma("sync", t.ap(), krr[:, :], "dbg")
    if "cq" in _DBG:
        t = dout("cq", (128, 4, TQ), BF16)
        ctx.wait("sync", "a_cq_evac", n_cq)
        ctx.dma("sync", t.ap(), cq[:, :, :], "dbg")

    if _STAGE >= 2:
        bes = ExitStack()
        salloc(bes, "of", [128, H, TQ], BF16)
        _p_attn_a_heads(ctx, nc, es, dr, sb, dout, salloc)
        bes.close()

    aes.close()


def _p_attn_a_heads(ctx, nc, es, dr, sb, dout, salloc):
    resid, ckv, krr, cq, of = sb["resid"], sb["ckv"], sb["krr"], sb["cq"], sb["of"]
    NKB = 8
    KB = 512
    SCL = 1.0 / float(np.sqrt(HD + PD))

    pps = ExitStack()
    o_ps = pps.enter_context(nc.psum_tensor("o_ps", [128, TQ], F32))
    sc_ps = [pps.enter_context(nc.psum_tensor(f"sc_ps{i}", [128, TQ], F32)) for i in range(2)]
    kc_ps = pps.enter_context(nc.psum_tensor("kc_ps", [128, KB], F32))
    vt_ps = pps.enter_context(nc.psum_tensor("vt_ps", [128, 4, 128], F32))

    hes = ExitStack()
    wukh = [hes.enter_context(nc.sbuf_tensor(f"wukh{i}", [128, 4, 128], BF16)) for i in range(2)]
    wuvh = [hes.enter_context(nc.sbuf_tensor(f"wuvh{i}", [128, 4, 128], BF16)) for i in range(2)]
    wuqh = [hes.enter_context(nc.sbuf_tensor(f"wuqh{i}", [128, 4, 128], BF16)) for i in range(2)]
    wqrh = [hes.enter_context(nc.sbuf_tensor(f"wqrh{i}", [128, 4, PD], BF16)) for i in range(2)]
    qch = hes.enter_context(nc.sbuf_tensor("qch", [128, TQ], BF16))
    qrh = hes.enter_context(nc.sbuf_tensor("qrh", [64, TQ], BF16))
    qrsw = hes.enter_context(nc.sbuf_tensor("qrsw", [64, TQ], BF16))
    rcown = hes.enter_context(nc.sbuf_tensor("rcownS", [64, TQ], F32))
    rsown = hes.enter_context(nc.sbuf_tensor("rsownS", [64, TQ], F32))
    kcb = [hes.enter_context(nc.sbuf_tensor(f"kcb{i}", [128, KB], BF16)) for i in range(2)]
    vtm = [hes.enter_context(nc.sbuf_tensor(f"vtm{i}", [128, 4, 128], BF16)) for i in range(2)]
    prb = [hes.enter_context(nc.sbuf_tensor(f"prb{i}", [128, TQ], BF16)) for i in range(2)]
    dacc = hes.enter_context(nc.sbuf_tensor("dacc", [128, TQ], F32))
    dinv = hes.enter_context(nc.sbuf_tensor("dinv", [128, TQ], F32))

    ctx.wait("gpsimd", "a_rope", 1)
    ctx.wait("sync", "a_rope", 1)
    ctx.wait("gpsimd", "a_cq_evac", ctx.cur("a_cq_evac"))
    ctx.wait("sync", "a_cq_evac", ctx.cur("a_cq_evac"))
    ctx.dma("sync", rcown[:, :], dr["rcown"].ap(), "h_rope_ld")
    ctx.dma("sync", rsown[:, :], dr["rsown"].ap(), "h_rope_ld")

    def head_weights(h, bi):
        hs = slice(h * 128, (h + 1) * 128)
        ctx.dma("gpsimd", wukh[bi][:, :, :],
                dr["wuk"].ap().rearrange("(lc p) d -> p lc d", p=128)[:, :, hs], "h_w")
        ctx.dma("gpsimd", wuvh[bi][:, :, :],
                dr["wuv"].ap().rearrange("(lc p) d -> p lc d", p=128)[:, :, hs], "h_w")
        ctx.dma("gpsimd", wuqh[bi][:, :, :],
                dr["wuq"].ap().rearrange("(lc p) d -> p lc d", p=128)[:, :, hs], "h_w")
        ctx.dma("gpsimd", wqrh[bi][:, :, :],
                dr["wqr"].ap().rearrange("(lc p) d -> p lc d", p=128)
                [:, :, h * PD:(h + 1) * PD], "h_w")

    head_weights(0, 0)

    for h in range(_NH):
        bi = h % 2
        if h + 1 < _NH:
            if h >= 1:
                ctx.wait("gpsimd", "h_kcmm", h * NKB)
                ctx.wait("gpsimd", "h_vtmm", h * NKB)
                ctx.wait("gpsimd", "h_qprep_mm", h)
            head_weights(h + 1, 1 - bi)

        ctx.wait("tensor", "h_w", 64 * (h + 1))
        ctx.wait("tensor", "a_cq_evac")
        if h >= 1:
            ctx.wait("tensor", "h_exp", h * NKB * 4)
            ctx.wait("tensor", "h_div", h)
        def qprep_mm(e, b=bi):
            for half in range(2):
                n = slice(half * 512, (half + 1) * 512)
                for lc in range(4):
                    e.matmul(sc_ps[0][:, n], wuqh[b][:, lc, :], cq[:, lc, n],
                             start=(lc == 0), stop=(lc == 3))
                for lc in range(4):
                    r = e.matmul(sc_ps[1][0:64, n], wqrh[b][:, lc, :], cq[:, lc, n],
                                 start=(lc == 0), stop=(lc == 3))
            return r
        ctx.op("tensor", qprep_mm, sem="h_qprep_mm")
        ctx.wait("vector", "h_qprep_mm", h + 1)
        def qprep_ev(e):
            e.tensor_scalar_mul(qch[:, :], sc_ps[0][:, :], SCL)
            return e.tensor_scalar_mul(qrh[:, :], sc_ps[1][0:64, :], SCL)
        ctx.op("vector", qprep_ev, sem="h_qprep_ev")
        ctx.wait("sync", "h_qprep_ev", h + 1)
        ctx.dma("sync", qrsw[0:32, :], qrh[32:64, :], "h_qsw")
        ctx.dma("sync", qrsw[32:64, :], qrh[0:32, :], "h_qsw")
        ctx.wait("vector", "h_qsw", 32 * (h + 1))
        if h == 0:
            ctx.wait("vector", "h_rope_ld")
        def qrope(e):
            e.tensor_mul(qrh[:, :], qrh[:, :], rcown[:, :])
            e.tensor_mul(qrsw[:, :], qrsw[:, :], rsown[:, :])
            return e.tensor_add(qrh[:, :], qrh[:, :], qrsw[:, :])
        ctx.op("vector", qrope, sem="h_qrope")

        for blk in range(NKB):
            ksl = slice(blk * KB, (blk + 1) * KB)
            ctx.wait("tensor", "h_kc_evac", ctx.cur("h_kc_evac"))
            def kc_mm(e, b=bi, k=ksl):
                for lc in range(4):
                    r = e.matmul(kc_ps[:, :], wukh[b][:, lc, :], ckv[:, lc, k],
                                 start=(lc == 0), stop=(lc == 3))
                return r
            ctx.op("tensor", kc_mm, sem="h_kcmm")
            ctx.wait("scalar", "h_kcmm", ctx.cur("h_kcmm"))
            ctx.wait("scalar", "h_scmm", max(0, ctx.cur("h_scmm") - 4))
            ctx.op("scalar", lambda e, b2=blk % 2: e.activation(kcb[b2][:, :], kc_ps[:, :], AF.Copy),
                   sem="h_kc_evac")
            ctx.wait("tensor", "h_vt_evac", ctx.cur("h_vt_evac"))
            def vt_mm(e, b=bi, k=ksl):
                for sub in range(4):
                    for lc in range(4):
                        r = e.matmul(vt_ps[:, sub, :],
                                     ckv[:, lc, k][:, sub * 128:(sub + 1) * 128],
                                     wuvh[b][:, lc, :],
                                     start=(lc == 0), stop=(lc == 3))
                return r
            ctx.op("tensor", vt_mm, sem="h_vtmm")
            ctx.wait("scalar", "h_vtmm", ctx.cur("h_vtmm"))
            ctx.wait("scalar", "h_avmm", max(0, ctx.cur("h_avmm") - 4))
            ctx.op("scalar", lambda e, b2=blk % 2: e.activation(vtm[b2][:, :, :], vt_ps[:, :, :], AF.Copy),
                   sem="h_vt_evac")

            for ktc in range(4):
                pi = ktc % 2
                kq = slice(blk * KB + ktc * 128, blk * KB + (ktc + 1) * 128)
                ctx.wait("tensor", "h_kc_evac", ctx.cur("h_kc_evac"))
                ctx.wait("tensor", "h_exp", max(0, ctx.cur("h_exp") - 1))
                def sc_mm(e, p=pi, kk=ktc, k=kq, b2=blk % 2):
                    for half in range(2):
                        n = slice(half * 512, (half + 1) * 512)
                        e.matmul(sc_ps[p][:, n], kcb[b2][:, kk * 128:(kk + 1) * 128],
                                 qch[:, n], start=True, stop=False)
                        r = e.matmul(sc_ps[p][:, n], krr[0:64, k], qrh[:, n],
                                     start=False, stop=True)
                    return r
                ctx.op("tensor", sc_mm, sem="h_scmm")
                ctx.wait("scalar", "h_scmm", ctx.cur("h_scmm"))
                ctx.wait("scalar", "h_dadd", max(0, ctx.cur("h_dadd") - 1))
                ctx.wait("scalar", "h_avmm", max(0, ctx.cur("h_avmm") - 1))
                ctx.op("scalar", lambda e, p=pi:
                       e.activation(prb[p][:, :], sc_ps[p][:, :], AF.Exp),
                       sem="h_exp")
                ctx.wait("vector", "h_exp", ctx.cur("h_exp"))
                first = (blk == 0 and ktc == 0)
                if first:
                    ctx.op("vector", lambda e, p=pi:
                           e.tensor_copy(dacc[:, :], prb[p][:, :]), sem="h_dadd")
                else:
                    ctx.op("vector", lambda e, p=pi:
                           e.tensor_add(dacc[:, :], dacc[:, :], prb[p][:, :]), sem="h_dadd")
                ctx.wait("tensor", "h_exp", ctx.cur("h_exp"))
                ctx.wait("tensor", "h_vt_evac", ctx.cur("h_vt_evac"))
                last = (blk == NKB - 1 and ktc == 3)
                def av_mm(e, p=pi, kk=ktc, f=first, l=last, b2=blk % 2):
                    for half in range(2):
                        n = slice(half * 512, (half + 1) * 512)
                        r = e.matmul(o_ps[:, n], vtm[b2][:, kk, :], prb[p][:, n],
                                     start=f, stop=l)
                    return r
                ctx.op("tensor", av_mm, sem="h_avmm")

        ctx.wait("tensor", "h_dadd", ctx.cur("h_dadd"))
        def d_mm(e):
            for half in range(2):
                n = slice(half * 512, (half + 1) * 512)
                r = e.matmul(sc_ps[0][0:1, n], sb["onesf"][:, :], dacc[:, n],
                             start=True, stop=True)
            return r
        ctx.op("tensor", d_mm, sem="h_dmm")
        ctx.wait("vector", "h_dmm", h + 1)
        ctx.op("vector", lambda e:
               e.reciprocal(dacc[0:1, 0:TQ], sc_ps[0][0:1, 0:TQ]), sem="h_drec")
        ctx.wait("tensor", "h_drec", h + 1)
        def dbc_mm(e):
            for half in range(2):
                n = slice(half * 512, (half + 1) * 512)
                r = e.matmul(sc_ps[1][:, n], sb["ones1f"][:, :], dacc[0:1, n],
                             start=True, stop=True)
            return r
        ctx.op("tensor", dbc_mm, sem="h_dbc")
        ctx.wait("vector", "h_dbc", h + 1)
        ctx.op("vector", lambda e: e.tensor_copy(dinv[:, :], sc_ps[1][:, :]), sem="h_dcp")
        ctx.wait("vector", "h_avmm", ctx.cur("h_avmm"))
        ctx.op("vector", lambda e, hh=h:
               e.tensor_mul(of[:, hh, :], o_ps[:, :], dinv[:, :]), sem="h_div")
        ctx.op("vector", lambda e, hh=h:
               e.tensor_copy(sb["drow"][0:1, hh:hh + 1], dinv[0:1, 0:1]), sem="h_drec2")

    n_div = ctx.cur("h_div")
    woes = ExitStack()
    wob = [woes.enter_context(nc.sbuf_tensor(f"wob{i}", [128, FC, 128], BF16))
           for i in range(3)]
    for fc in range(FC):
        bi = fc % 3
        ctx.wait("gpsimd", "h_wo_mm", max(0, fc - 2))
        ctx.dma("gpsimd", wob[bi][:, :, :],
                dr["wo"].ap().rearrange("(dc p) f -> p dc f", p=128)
                [:, :, fc * 128:(fc + 1) * 128], "h_wo_ld")
        ctx.wait("tensor", "h_wo_ld", 16 * (fc + 1))
        if fc == 0:
            ctx.wait("tensor", "h_div", n_div)
        ctx.wait("tensor", "h_wo_add", max(0, fc - 1))
        pb2 = fc % 2
        def wo_mm(e, b=bi, p=pb2):
            for half in range(2):
                n = slice(half * 512, (half + 1) * 512)
                for dc in range(FC):
                    r = e.matmul(sc_ps[p][:, n], wob[b][:, dc, :], of[:, dc, n],
                                 start=(dc == 0), stop=(dc == FC - 1))
            return r
        ctx.op("tensor", wo_mm, sem="h_wo_mm")
        ctx.wait("vector", "h_wo_mm", fc + 1)
        ctx.op("vector", lambda e, p=pb2, f=fc:
               e.tensor_add(resid[:, f, :], resid[:, f, :], sc_ps[p][:, :]),
               sem="h_wo_add")
    woes.close()
    ctx.wait("sync", "h_wo_add", ctx.cur("h_wo_add"))
    ctx.op("sync", lambda e: e.nop(), sem="resid_ok")

    if "x1" in _DBG:
        t = dout("x1", (128, FC, TQ), F32)
        ctx.wait("sync", "h_wo_add", ctx.cur("h_wo_add"))
        ctx.dma("sync", t.ap(), resid[:, :, :], "dbg")
    for nm, tile, shp, dt_, sem in [
        ("of", of[:, :, :], (128, H, TQ), BF16, "h_div"),
        ("drow", sb["drow"][:, :], (1, 16), F32, "h_drec2"),
        ("qch", qch[:, :], (128, TQ), BF16, "h_div"),
        ("qrh", qrh[:, :], (64, TQ), BF16, "h_div"),
        ("dinv", dinv[:, :], (128, TQ), F32, "h_div"),
        ("dacc", dacc[:, :], (128, TQ), F32, "h_div"),
        ("kcb", kcb[0][:, :], (128, 512), BF16, "h_div"),
        ("prb", prb[1][:, :], (128, TQ), BF16, "h_div"),
    ]:
        if nm in _DBG:
            t = dout(nm, shp, dt_)
            ctx.wait("sync", sem, ctx.cur(sem))
            ctx.dma("sync", t.ap(), tile, "dbg")

    pps.close()
    hes.close()


def _emit_ln(ctx, nc, sb, prefix, src_ap, sqt, htile, ps_st, ps_bc, strow):
    Tn = htile.shape[2]
    ctx.wait("scalar", prefix + "_ln", ctx.cur(prefix + "_ln"))
    ctx.op("scalar", lambda e: e.activation(sqt[:, :, :], src_ap, AF.Copy),
           sem=prefix + "_ln")
    ctx.wait("tensor", prefix + "_ln", ctx.cur(prefix + "_ln"))
    def sum_mm(e):
        for kc in range(FC):
            r = e.matmul(ps_st[:, 0, :], sb["onesb"][:, :], sqt[:, kc, :],
                         start=(kc == 0), stop=(kc == FC - 1))
        return r
    ctx.op("tensor", sum_mm, sem=prefix + "_ln")
    ctx.wait("vector", prefix + "_ln", ctx.cur(prefix + "_ln"))
    ctx.op("vector", lambda e: e.tensor_mul(sqt[:, :, :], sqt[:, :, :], sqt[:, :, :]),
           sem=prefix + "_ln")
    ctx.wait("tensor", prefix + "_ln", ctx.cur(prefix + "_ln"))
    def sq_mm(e):
        for kc in range(FC):
            r = e.matmul(ps_st[:, 1, :], sb["onesb"][:, :], sqt[:, kc, :],
                         start=(kc == 0), stop=(kc == FC - 1))
        return r
    ctx.op("tensor", sq_mm, sem=prefix + "_ln")
    ctx.wait("vector", prefix + "_ln", ctx.cur(prefix + "_ln"))
    def statcalc(e):
        e.tensor_scalar_mul(strow[:, 0, :], ps_st[:, 0, :], 1.0 / D)
        e.tensor_scalar_mul(strow[:, 2, :], ps_st[:, 1, :], 1.0 / D)
        e.tensor_mul(strow[:, 1, :], strow[:, 0, :], strow[:, 0, :])
        e.tensor_sub(strow[:, 2, :], strow[:, 2, :], strow[:, 1, :])
        return e.tensor_scalar_add(strow[:, 2, :], strow[:, 2, :], 1e-5)
    ctx.op("vector", statcalc, sem=prefix + "_ln")
    ctx.wait("scalar", prefix + "_ln", ctx.cur(prefix + "_ln"))
    ctx.op("scalar", lambda e: e.activation(strow[:, 2, :], strow[:, 2, :], AF.Sqrt),
           sem=prefix + "_ln")
    ctx.wait("vector", prefix + "_ln", ctx.cur(prefix + "_ln"))
    ctx.op("vector", lambda e: e.reciprocal(strow[:, 1, :], strow[:, 2, :]),
           sem=prefix + "_ln")
    ctx.wait("tensor", prefix + "_ln", ctx.cur(prefix + "_ln"))
    def bcast(e):
        e.matmul(ps_bc[:, 0, :], sb["ones1f"][:, :], strow[:, 0, :], start=True, stop=True)
        return e.matmul(ps_bc[:, 1, :], sb["ones1f"][:, :], strow[:, 1, :],
                        start=True, stop=True)
    ctx.op("tensor", bcast, sem=prefix + "_ln")
    ctx.wait("vector", prefix + "_ln", ctx.cur(prefix + "_ln"))
    def happly(e):
        mb = ps_bc[:, 0:1, :].broadcast_to([128, FC, Tn])
        rb = ps_bc[:, 1:2, :].broadcast_to([128, FC, Tn])
        e.tensor_sub(htile[:, :, :], src_ap, mb)
        return e.tensor_mul(htile[:, :, :], htile[:, :, :], rb)
    ctx.op("vector", happly, sem=prefix + "_ln")


def _p_ffn(ctx, nc, es, dr, sb, dout, salloc, tag):
    resid = sb["resid"]
    OGW = 256
    NOG = FF // OGW
    NHC = FF // 128

    fes = ExitStack()
    htile = fes.enter_context(nc.sbuf_tensor(tag + "h", [128, FC, 512], BF16))
    strow = fes.enter_context(nc.sbuf_tensor(tag + "st", [1, 3, 512], F32))
    x2 = fes.enter_context(nc.sbuf_tensor(tag + "x2", [128, NHC, 512], BF16))
    hA = fes.enter_context(nc.sbuf_tensor(tag + "hA", [16, 512], BF16))
    x2A = fes.enter_context(nc.sbuf_tensor(tag + "x2A", [16, 512], BF16))
    aupt = fes.enter_context(nc.sbuf_tensor(tag + "aup", [128, FC, LR], BF16))
    adnt = fes.enter_context(nc.sbuf_tensor(tag + "adn", [128, NHC, LR], BF16))
    bupt = fes.enter_context(nc.sbuf_tensor(tag + "bup", [128, NHC], F32))
    bdnt = fes.enter_context(nc.sbuf_tensor(tag + "bdn", [128, FC], F32))

    pps = ExitStack()
    ps_bc = pps.enter_context(nc.psum_tensor(tag + "pbc", [128, 2, 512], F32))
    ps_st = ps_bc[0:1, :, :]
    ps_y1 = [pps.enter_context(nc.psum_tensor(tag + f"py1{i}", [128, 512], F32))
             for i in range(2)]
    ps_hA = pps.enter_context(nc.psum_tensor(tag + "phA", [16, 512], F32))
    ps_y2 = [pps.enter_context(nc.psum_tensor(tag + f"py2{i}", [128, 512], F32))
             for i in range(2)]

    pfx = tag
    nro = ctx.cur("resid_ok")
    ctx.wait("gpsimd", "resid_ok", nro)
    ctx.wait("sync", "resid_ok", nro)
    ctx.dma("gpsimd", aupt[:, :, :], dr["aup"].ap().rearrange("(c p) r -> p c r", p=128),
            pfx + "_cld")
    ctx.dma("gpsimd", adnt[:, :, :], dr["adn"].ap().rearrange("(c p) r -> p c r", p=128),
            pfx + "_cld")
    ctx.dma("sync", bupt[:, :], dr["bup"].ap(), pfx + "_cld")
    ctx.dma("sync", bdnt[:, :], dr["bdn"].ap(), pfx + "_cld")
    n_cld = ctx.cur(pfx + "_cld")

    for ps in range(2):
        tsl = slice(ps * 512, (ps + 1) * 512)
        ctx.wait("scalar", "resid_ok", nro)
        if ps == 1:
            ctx.wait("scalar", pfx + "_y2add", ctx.cur(pfx + "_y2add"))
            ctx.wait("scalar", pfx + "_ln", ctx.cur(pfx + "_ln"))
        lns = ExitStack()
        sqt = lns.enter_context(nc.sbuf_tensor(tag + f"sq{ps}", [128, FC, 512], BF16))
        _emit_ln(ctx, nc, sb, pfx, resid[:, :, tsl], sqt, htile, ps_st, ps_bc, strow)
        lns.close()

        ctx.wait("tensor", pfx + "_ln", ctx.cur(pfx + "_ln"))
        if ps == 0:
            ctx.wait("tensor", pfx + "_cld", n_cld)
        ctx.wait("tensor", pfx + "_hAe", ctx.cur(pfx + "_hAe"))
        ctx.wait("tensor", pfx + "_x2Ae", ctx.cur(pfx + "_x2Ae"))
        def hA_mm(e):
            for kc in range(FC):
                r = e.matmul(ps_hA[:, :], aupt[:, kc, :], htile[:, kc, :],
                             start=(kc == 0), stop=(kc == FC - 1))
            return r
        ctx.op("tensor", hA_mm, sem=pfx + "_hAmm")
        ctx.wait("scalar", pfx + "_hAmm", ctx.cur(pfx + "_hAmm"))
        ctx.op("scalar", lambda e: e.activation(hA[:, :], ps_hA[:, :], AF.Copy),
               sem=pfx + "_hAe")

        ctx.wait("gpsimd", pfx + "_qmm", ctx.cur(pfx + "_qmm"))
        ctx.wait("gpsimd", pfx + "_ln", ctx.cur(pfx + "_ln"))
        ups = ExitStack()
        wq8 = [ups.enter_context(nc.sbuf_tensor(tag + f"wq8_{ps}_{i}", [128, FC, OGW], I8))
               for i in range(2)]
        sct = ups.enter_context(nc.sbuf_tensor(tag + f"sct_{ps}", [128, FC, OGW], F32))
        wbb = [ups.enter_context(nc.sbuf_tensor(tag + f"wbb_{ps}_{i}", [128, FC, OGW], BF16))
               for i in range(3)]
        bplb = [ups.enter_context(nc.sbuf_tensor(tag + f"bpl_{ps}_{i}", [16, OGW], BF16))
                for i in range(2)]
        for og in range(NOG):
            gb = og % 2
            osl = slice(og * OGW, (og + 1) * OGW)
            if _PAR_A:
                ctx.wait("gpsimd", pfx + "_y1ev", ctx.cur(pfx + "_y1ev"))
            if og >= 2:
                ctx.wait("gpsimd", pfx + "_dq", ctx.cur(pfx + "_dq") - 1)
            ctx.dma("gpsimd", wq8[gb][:, :, :], dr["wupT"].ap()[og:og + 1].squeeze(0), pfx + "_wld")
            if og >= 1:
                ctx.wait("gpsimd", pfx + "_dq", ctx.cur(pfx + "_dq"))
            for pg in range(2):
                ctx.dma("gpsimd", sct[pg * 64:(pg + 1) * 64, :, :],
                        dr["supT"].ap().rearrange("(k g) o -> g k o", g=2)[pg][:, osl]
                        .unsqueeze(0).broadcast_to([64, FC, OGW]), pfx + "_wld")
            if og >= 2:
                ctx.wait("gpsimd", pfx + "_y1mm", ctx.cur(pfx + "_y1mm") - 1)
            ctx.dma("gpsimd", bplb[gb][:, :], dr["bupl"].ap()[:, osl], pfx + "_wld")
            ctx.wait("vector", pfx + "_wld", ctx.cur(pfx + "_wld"))
            ctx.wait("vector", pfx + "_y1mm", ctx.cur(pfx + "_y1mm"))
            ctx.op("vector", lambda e, g=gb, g3=og % 3:
                   e.scalar_tensor_tensor(wbb[g3][:, :, :], wq8[g][:, :, :], -7.5,
                                          sct[:, :, :], ALU.add, ALU.mult),
                   sem=pfx + "_dq")
            ctx.wait("tensor", pfx + "_dq", ctx.cur(pfx + "_dq"))
            ctx.wait("tensor", pfx + "_hAe", ctx.cur(pfx + "_hAe"))
            if og >= 1:
                ctx.wait("tensor", pfx + "_y1ev", ctx.cur(pfx + "_y1ev"))
            def y1_mm(e, g3=og % 3, g=gb):
                for oc2 in range(2):
                    for kc in range(FC):
                        e.matmul(ps_y1[oc2][:, :],
                                 wbb[g3][:, kc, oc2 * 128:(oc2 + 1) * 128],
                                 htile[:, kc, :], start=(kc == 0), stop=False)
                    r = e.matmul(ps_y1[oc2][:, :],
                                 bplb[g][:, oc2 * 128:(oc2 + 1) * 128],
                                 hA[:, :], start=False, stop=True)
                return r
            ctx.op("tensor", y1_mm, sem=pfx + "_y1mm")
            ctx.wait("scalar", pfx + "_y1mm", ctx.cur(pfx + "_y1mm"))
            if og == 0 and ps == 0:
                ctx.wait("scalar", pfx + "_cld", n_cld)
            def y1_ev(e, o=og):
                for oc2 in range(2):
                    oc = o * 2 + oc2
                    r = e.activation(x2[:, oc, :], ps_y1[oc2][:, :], AF.Relu,
                                     bias=bupt[:, oc:oc + 1])
                return r
            ctx.op("scalar", y1_ev, sem=pfx + "_y1ev", n=2)
        ups.close()

        ctx.wait("tensor", pfx + "_y1ev", ctx.cur(pfx + "_y1ev"))
        ctx.wait("tensor", pfx + "_hAe", ctx.cur(pfx + "_hAe"))
        if ps == 1:
            ctx.wait("tensor", pfx + "_x2Ae", 1)
        def x2A_mm(e):
            for hc in range(NHC):
                r = e.matmul(ps_hA[:, :], adnt[:, hc, :], x2[:, hc, :],
                             start=(hc == 0), stop=(hc == NHC - 1))
            return r
        ctx.op("tensor", x2A_mm, sem=pfx + "_x2Amm")
        ctx.wait("scalar", pfx + "_x2Amm", ctx.cur(pfx + "_x2Amm"))
        ctx.op("scalar", lambda e: e.activation(x2A[:, :], ps_hA[:, :], AF.Copy),
               sem=pfx + "_x2Ae")

        HH = NHC // 2
        ctx.wait("gpsimd", pfx + "_y1mm", ctx.cur(pfx + "_y1mm"))
        dns = ExitStack()
        dq8 = [dns.enter_context(nc.sbuf_tensor(tag + f"dq8_{ps}_{i}", [128, HH, 128], I8))
               for i in range(2)]
        sc2 = dns.enter_context(nc.sbuf_tensor(tag + f"sc2_{ps}", [128, HH, 128], F32))
        wbd = [dns.enter_context(nc.sbuf_tensor(tag + f"wbd_{ps}_{i}", [128, HH, 128], BF16))
               for i in range(3)]
        bdlb = [dns.enter_context(nc.sbuf_tensor(tag + f"bdl_{ps}_{i}", [16, 128], BF16))
                for i in range(2)]
        for dc in range(FC):
            dsl = slice(dc * 128, (dc + 1) * 128)
            pb = dc % 2
            for hh in range(2):
                u = dc * 2 + hh
                gb = u % 2
                if _PAR_B:
                    ctx.wait("gpsimd", pfx + "_y2add", ctx.cur(pfx + "_y2add"))
                    ctx.wait("gpsimd", pfx + "_y2mm", ctx.cur(pfx + "_y2mm"))
                if u >= 2:
                    ctx.wait("gpsimd", pfx + "_ddq", ctx.cur(pfx + "_ddq") - 1)
                ctx.dma("gpsimd", dq8[gb][:, :, :],
                        dr["wdnT"].ap()[dc:dc + 1].squeeze(0)[:, hh * HH:(hh + 1) * HH, :],
                        pfx + "_dld")
                ctx.wait("gpsimd", pfx + "_ddq", ctx.cur(pfx + "_ddq"))
                for pg in range(2):
                    ctx.dma("gpsimd", sc2[pg * 64:(pg + 1) * 64, :, :],
                            dr["sdnT"].ap().rearrange("(k g) o -> g k o", g=2)[pg]
                            [hh * HH:(hh + 1) * HH, dsl]
                            .unsqueeze(0).broadcast_to([64, HH, 128]), pfx + "_dld")
                if hh == 0:
                    ctx.wait("gpsimd", pfx + "_y2mm", max(0, ctx.cur(pfx + "_y2mm") - 2))
                    ctx.dma("gpsimd", bdlb[pb][:, :], dr["bdnl"].ap()[:, dsl],
                            pfx + "_dld")
                ctx.wait("vector", pfx + "_dld", ctx.cur(pfx + "_dld"))
                ctx.wait("vector", pfx + "_y2mm", ctx.cur(pfx + "_y2mm"))
                ctx.op("vector", lambda e, g=gb, g3=u % 3:
                       e.scalar_tensor_tensor(wbd[g3][:, :, :], dq8[g][:, :, :], -7.5,
                                              sc2[:, :, :], ALU.add, ALU.mult),
                       sem=pfx + "_ddq")
                ctx.wait("tensor", pfx + "_ddq", ctx.cur(pfx + "_ddq"))
                if u == 0:
                    ctx.wait("tensor", pfx + "_x2Ae", ctx.cur(pfx + "_x2Ae"))
                if hh == 0 and dc >= 2:
                    ctx.wait("tensor", pfx + "_y2add", ctx.cur(pfx + "_y2add") - 1)
                def y2_mm(e, g3=u % 3, d=dc, h2=hh, p=pb):
                    for hc in range(HH):
                        r = e.matmul(ps_y2[p][:, :], wbd[g3][:, hc, :],
                                     x2[:, h2 * HH + hc, :],
                                     start=(h2 == 0 and hc == 0), stop=False)
                    if h2 == 1:
                        r = e.matmul(ps_y2[p][:, :], bdlb[p][:, :], x2A[:, :],
                                     start=False, stop=True)
                    return r
                ctx.op("tensor", y2_mm, sem=pfx + "_y2mm")
            ctx.wait("vector", pfx + "_y2mm", ctx.cur(pfx + "_y2mm"))
            if dc == 0 and ps == 0:
                ctx.wait("vector", pfx + "_cld", n_cld)
            ctx.op("vector", lambda e, d=dc, t=tsl, p=pb:
                   e.scalar_tensor_tensor(resid[:, d, t], ps_y2[p][:, :],
                                          bdnt[:, d:d + 1],
                                          resid[:, d, t], ALU.add, ALU.add),
                   sem=pfx + "_y2add")
        dns.close()

    if tag + "x2" in _DBG:
        t = dout(tag + "x2", (128, FF // 128, 512), BF16)
        ctx.wait("sync", pfx + "_y1ev", ctx.cur(pfx + "_y1ev"))
        ctx.dma("sync", t.ap(), x2[:, :, :], "dbg")
    if tag + "hd" in _DBG:
        t = dout(tag + "hd", (128, FC, 512), BF16)
        ctx.wait("sync", pfx + "_ln", ctx.cur(pfx + "_ln"))
        ctx.dma("sync", t.ap(), htile[:, :, :], "dbg")
    ctx.wait("sync", pfx + "_y2add", ctx.cur(pfx + "_y2add"))
    ctx.op("sync", lambda e: e.nop(), sem="resid_ok")
    if tag + "res" in _DBG:
        t = dout(tag + "res", (128, FC, TQ), F32)
        ctx.wait("sync", pfx + "_y2add", ctx.cur(pfx + "_y2add"))
        ctx.dma("sync", t.ap(), resid[:, :, :], "dbg")
    pps.close()
    fes.close()


def _p_attn_b(ctx, nc, es, dr, sb, dout, salloc):
    resid = sb["resid"]
    pfx = "b"

    fes = ExitStack()
    htile = fes.enter_context(nc.sbuf_tensor("bh", [128, FC, TQ], BF16))
    ofb = fes.enter_context(nc.sbuf_tensor("bof", [128, H, TQ], BF16))
    kvst = fes.enter_context(nc.sbuf_tensor("bkv", [128, 8, 128], F32))
    strow = fes.enter_context(nc.sbuf_tensor("bst", [1, 3, 512], F32))
    qplr2 = fes.enter_context(nc.sbuf_tensor("bqp2", [1, 2, 512], F32))
    negs = fes.enter_context(nc.sbuf_tensor("bns", [1, 16], F32))
    negsb = fes.enter_context(nc.sbuf_tensor("bnsb", [128, 16], F32))
    bdp = fes.enter_context(nc.sbuf_tensor("bbdp", [128, 4, H], F32))
    kdeg = fes.enter_context(nc.sbuf_tensor("bkdg", [128, 2], F32))
    kde = fes.enter_context(nc.sbuf_tensor("bkde", [128, 2], F32))
    dge = fes.enter_context(nc.sbuf_tensor("bdge", [128, 2, 256], BF16))
    dgm = fes.enter_context(nc.sbuf_tensor("bdgm", [128, 2, 256], BF16))
    decT = fes.enter_context(nc.sbuf_tensor("bdec", [128, 2, 256], F32))
    pmt = fes.enter_context(nc.sbuf_tensor("bpm", [128, 4], F32))
    pc2 = fes.enter_context(nc.sbuf_tensor("bpc2", [1, 2, 64], F32))
    ccoef = fes.enter_context(nc.sbuf_tensor("bcc", [128, 64], F32))

    nro = ctx.cur("resid_ok")
    ctx.wait("gpsimd", "resid_ok", nro)
    ctx.wait("sync", "resid_ok", nro)
    ctx.dma("sync", kdeg[:, :], dr["kdecg"].ap(), pfx + "_cld")
    ctx.dma("gpsimd", dge[:, :, :], dr["dgexp"].ap().rearrange("g p o -> p g o"), pfx + "_cldg")
    ctx.dma("gpsimd", dgm[:, :, :], dr["dgmask"].ap().rearrange("g p o -> p g o"), pfx + "_cldg")
    ctx.dma("sync", qplr2[:, :, :], dr["qposl"].ap().rearrange("a (b t) -> a b t", b=2), pfx + "_cld")
    ctx.dma("sync", pmt[:, :], dr["pmask"].ap(), pfx + "_cld")
    ctx.dma("sync", pc2[:, 0, 0:4], dr["pcoefg"].ap(), pfx + "_cld")
    ctx.dma("sync", pc2[:, 1, 0:4], dr["pcmask"].ap(), pfx + "_cld")
    n_cld = ctx.cur(pfx + "_cld")

    pps0 = ExitStack()
    ps_bc = pps0.enter_context(nc.psum_tensor("bpbc", [128, 2, 512], F32))
    ps_st = ps_bc[0:1, :, :]

    ctx.dma("sync", negs[:, :], dr["slope"].ap(), pfx + "_cld")
    n_cld = ctx.cur(pfx + "_cld")
    ctx.wait("vector", pfx + "_cld", n_cld)
    ctx.op("vector", lambda e: e.tensor_scalar_mul(negs[:, :], negs[:, :], -1.0),
           sem=pfx + "_neg")
    ctx.wait("tensor", pfx + "_neg", ctx.cur(pfx + "_neg"))
    ctx.op("tensor", lambda e: e.matmul(ps_bc[:, 0, 0:16], sb["ones1f"][:, :],
                                        negs[:, :], start=True, stop=True),
           sem=pfx + "_nbc")
    ctx.wait("vector", pfx + "_nbc", ctx.cur(pfx + "_nbc"))
    ctx.op("vector", lambda e: e.tensor_copy(negsb[:, :], ps_bc[:, 0, 0:16]),
           sem=pfx + "_nsb")
    ctx.wait("vector", pfx + "_nsb", ctx.cur(pfx + "_nsb"))
    def bdrow(e):
        for l in range(4):
            r = e.tensor_scalar_mul(strow[:, 0, l * 16:(l + 1) * 16],
                                    negs[:, :], 256.0 * l)
        return r
    ctx.op("vector", bdrow, sem=pfx + "_bdr")
    ctx.wait("scalar", pfx + "_bdr", ctx.cur(pfx + "_bdr"))
    ctx.op("scalar", lambda e: e.activation(strow[:, 0, 0:64], strow[:, 0, 0:64], AF.Exp),
           sem=pfx + "_bde")
    ctx.wait("tensor", pfx + "_bde", ctx.cur(pfx + "_bde"))
    ctx.op("tensor", lambda e: e.matmul(ps_bc[:, 0, 0:64], sb["ones1f"][:, :],
                                        strow[:, 0, 0:64], start=True, stop=True),
           sem=pfx + "_bdb")
    ctx.wait("vector", pfx + "_bdb", ctx.cur(pfx + "_bdb"))
    ctx.op("vector", lambda e: e.tensor_copy(
        bdp[:, :, :].rearrange("p l h -> p (l h)"), ps_bc[:, 0, 0:64]), sem=pfx + "_bdp")
    def pcrow(e):
        for p in range(4):
            e.tensor_scalar_mul(strow[:, 1, p * 16:(p + 1) * 16], negs[:, :], 1.0)
            r = e.tensor_single_scalar(strow[:, 1, p * 16:(p + 1) * 16],
                                       strow[:, 1, p * 16:(p + 1) * 16],
                                       pc2[:, 0, p:p + 1], ALU.mult)
        return r
    ctx.op("vector", pcrow, sem=pfx + "_pcr")
    ctx.wait("scalar", pfx + "_pcr", ctx.cur(pfx + "_pcr"))
    ctx.op("scalar", lambda e: e.activation(strow[:, 1, 0:64], strow[:, 1, 0:64], AF.Exp),
           sem=pfx + "_pce")
    ctx.wait("vector", pfx + "_pce", ctx.cur(pfx + "_pce"))
    def pcmask_mul(e):
        for p in range(4):
            r = e.tensor_single_scalar(strow[:, 1, p * 16:(p + 1) * 16],
                                       strow[:, 1, p * 16:(p + 1) * 16],
                                       pc2[:, 1, p:p + 1], ALU.mult)
        return r
    ctx.op("vector", pcmask_mul, sem=pfx + "_pcm")
    ctx.wait("tensor", pfx + "_pcm", ctx.cur(pfx + "_pcm"))
    ctx.op("tensor", lambda e: e.matmul(ps_bc[:, 1, 0:64], sb["ones1f"][:, :],
                                        strow[:, 1, 0:64], start=True, stop=True),
           sem=pfx + "_ccb")
    ctx.wait("vector", pfx + "_ccb", ctx.cur(pfx + "_ccb"))
    ctx.op("vector", lambda e: e.tensor_copy(ccoef[:, :], ps_bc[:, 1, 0:64]),
           sem=pfx + "_cc")

    lns = ExitStack()
    sqt = lns.enter_context(nc.sbuf_tensor("bsq", [128, FC, 512], BF16))
    ctx.wait("tensor", pfx + "_cc", ctx.cur(pfx + "_cc"))
    ctx.wait("vector", pfx + "_cc", ctx.cur(pfx + "_cc"))
    for hf in range(2):
        tsl = slice(hf * 512, (hf + 1) * 512)
        ctx.wait("scalar", "resid_ok", nro)
        _emit_ln(ctx, nc, sb, pfx + f"ln{hf}", resid[:, :, tsl], sqt,
                 htile[:, :, tsl], ps_st, ps_bc, strow)
    ctx.wait("scalar", pfx + "ln1_ln", ctx.cur(pfx + "ln1_ln"))
    ctx.op("scalar", lambda e: e.activation(htile[:, :, :], htile[:, :, :], AF.Copy),
           sem=pfx + "_hs")
    lns.close()
    pps0.close()

    if _BCUT <= 1:
        fes.close()
        return

    HG = 8
    kvgd = [nc.dram_tensor(f"kvgd{g}", [4, 128, HG * 128], F32, kind="Internal")
            for g in range(2)]
    qes = ExitStack()
    qdq = qes.enter_context(nc.sbuf_tensor("bqdq", [128, HG, TQ], BF16))
    wqs = [qes.enter_context(nc.sbuf_tensor(f"bwq{i}", [128, FC, 3 * HD], BF16))
           for i in range(2)]
    qf = qes.enter_context(nc.sbuf_tensor("bqf", [128, TQ], BF16))
    kf = qes.enter_context(nc.sbuf_tensor("bkf", [128, TQ], BF16))
    vt = qes.enter_context(nc.sbuf_tensor("bvt", [128, 8, 128], BF16))
    ktm = qes.enter_context(nc.sbuf_tensor("bktm", [128, 8, 128], BF16))
    ktp = qes.enter_context(nc.sbuf_tensor("bktp", [128, 2, 128], BF16))
    prbB = qes.enter_context(nc.sbuf_tensor("bprb", [128, 2, 256], BF16))
    kvbf = qes.enter_context(nc.sbuf_tensor("bkvb", [128, 128], BF16))
    kvin = qes.enter_context(nc.sbuf_tensor("bkvi", [128, HG, 128], F32))
    kstg = qes.enter_context(nc.sbuf_tensor("bkst", [128, HG * 128], F32))

    pps = ExitStack()
    ps_a = [pps.enter_context(nc.psum_tensor(f"bpa{i}", [128, 512], F32)) for i in range(2)]
    ps_sc = pps.enter_context(nc.psum_tensor("bpsc", [128, 2, 256], F32))
    ps_o = pps.enter_context(nc.psum_tensor("bpo", [128, 256], F32))
    ps_kv = pps.enter_context(nc.psum_tensor("bpkv", [128, 128], F32))
    ps_vt = pps.enter_context(nc.psum_tensor("bpvt", [128, 128], F32))
    ps_tr = pps.enter_context(nc.psum_tensor("bptr", [128, 128], BF16))

    def head_w(h, bi):
        base = h * 3 * HD
        ctx.dma("gpsimd", wqs[bi][:, :, :],
                dr["wqkvT"].ap().rearrange("(c p) o -> p c o", p=128)
                [:, :, base:base + 3 * HD], pfx + "_w")

    head_w(0, 0)

    for hg in range(2):
        for hl in range(HG):
            h = hg * HG + hl
            bi = h % 2
            if h + 1 < H:
                if h >= 1:
                    ctx.wait("gpsimd", pfx + "_qkmm", 2 * h)
                    ctx.wait("gpsimd", pfx + "_vtmm", 8 * h)
                head_w(h + 1, 1 - bi)

            ctx.wait("tensor", pfx + "_w", 16 * (h + 1))
            if h == 0:
                ctx.wait("tensor", pfx + "_hs", 1)
                ctx.wait("tensor", pfx + "_cc", ctx.cur(pfx + "_cc"))
            if h >= 1:
                ctx.wait("tensor", pfx + "_qfe", 2 * h)
                ctx.wait("tensor", pfx + "_qdq", h)
            def qk_mm(e, b=bi, which=0):
                r = None
                for half in range(2):
                    n = slice(half * 512, (half + 1) * 512)
                    for kc in range(FC):
                        r = e.matmul(ps_a[half][:, :], wqs[b][:, kc, which * HD:(which + 1) * HD],
                                     htile[:, kc, n], start=(kc == 0), stop=(kc == FC - 1))
                return r
            ctx.op("tensor", qk_mm, sem=pfx + "_qkmm")
            ctx.wait("scalar", pfx + "_qkmm", ctx.cur(pfx + "_qkmm"))
            def qf_ev(e):
                e.activation(qf[:, 0:512], ps_a[0][:, :], AF.Silu)
                return e.activation(qf[:, 512:1024], ps_a[1][:, :], AF.Silu)
            ctx.op("scalar", qf_ev, sem=pfx + "_qfe", n=1)
            ctx.wait("tensor", pfx + "_qfe", ctx.cur(pfx + "_qfe"))
            def kf_mm(e, b=bi):
                return qk_mm(e, b, 1)
            ctx.op("tensor", kf_mm, sem=pfx + "_qkmm")
            ctx.wait("scalar", pfx + "_qkmm", ctx.cur(pfx + "_qkmm"))
            def kf_ev(e):
                e.activation(kf[:, 0:512], ps_a[0][:, :], AF.Silu)
                return e.activation(kf[:, 512:1024], ps_a[1][:, :], AF.Silu)
            ctx.op("scalar", kf_ev, sem=pfx + "_qfe", n=1)

            for tc in range(8):
                ctx.wait("tensor", pfx + "_vte", ctx.cur(pfx + "_vte"))
                def vt_mm(e, b=bi, t=tc):
                    r = None
                    for kc in range(FC):
                        r = e.matmul(ps_vt[:, :], htile[:, kc, t * 128:(t + 1) * 128],
                                     wqs[b][:, kc, 2 * HD:3 * HD],
                                     start=(kc == 0), stop=(kc == FC - 1))
                    return r
                ctx.op("tensor", vt_mm, sem=pfx + "_vtmm")
                ctx.wait("scalar", pfx + "_vtmm", ctx.cur(pfx + "_vtmm"))
                ctx.op("scalar", lambda e, t=tc:
                       e.activation(vt[:, t, :], ps_vt[:, :], AF.Silu), sem=pfx + "_vte")

            ctx.wait("tensor", pfx + "_qfe", ctx.cur(pfx + "_qfe"))
            for tc in range(8):
                ctx.wait("tensor", pfx + "_kte", ctx.cur(pfx + "_kte"))
                ctx.op("tensor", lambda e, t=tc:
                       e.transpose(ps_tr[:, :], kf[:, t * 128:(t + 1) * 128],
                                   sb["identt"][:, :]), sem=pfx + "_ktr")
                ctx.wait("scalar", pfx + "_ktr", ctx.cur(pfx + "_ktr"))
                ctx.op("scalar", lambda e, t=tc:
                       e.activation(ktm[:, t, :], ps_tr[:, :], AF.Copy), sem=pfx + "_kte")

            ctx.wait("scalar", pfx + "_prb", ctx.cur(pfx + "_prb"))
            if h == 0:
                ctx.wait("scalar", pfx + "_nsb", ctx.cur(pfx + "_nsb"))
                ctx.wait("scalar", pfx + "_cldg", ctx.cur(pfx + "_cldg"))
            ctx.op("scalar", lambda e, hh=h:
                   e.activation(decT[:, :, :], dge[:, :, :], AF.Exp,
                                scale=negsb[:, hh:hh + 1]), sem=pfx + "_dce")
            ctx.wait("scalar", pfx + "_kvmm", ctx.cur(pfx + "_kvmm"))
            ctx.op("scalar", lambda e, hh=h:
                   e.activation(kde[:, :], kdeg[:, :], AF.Exp, scale=negsb[:, hh:hh + 1]),
                   sem=pfx + "_kdee")
            ctx.wait("vector", pfx + "_dce", ctx.cur(pfx + "_dce"))
            ctx.wait("vector", pfx + "_prb", ctx.cur(pfx + "_prb"))
            ctx.op("vector", lambda e: e.tensor_mul(decT[:, :, :], decT[:, :, :],
                                                    dgm[:, :, :]), sem=pfx + "_dcm")
            ctx.wait("scalar", pfx + "_qdb", ctx.cur(pfx + "_qdb"))
            ctx.op("scalar", lambda e, hh=h:
                   e.activation(strow[:, 1, :], qplr2[:, 0, :],
                                AF.Exp, scale=negs[:, hh:hh + 1]), sem=pfx + "_qde")
            ctx.op("scalar", lambda e, hh=h:
                   e.activation(strow[:, 2, :], qplr2[:, 1, :], AF.Exp,
                                scale=negs[:, hh:hh + 1]), sem=pfx + "_qde")
            ctx.wait("tensor", pfx + "_qde", ctx.cur(pfx + "_qde"))
            ctx.wait("tensor", pfx + "_qkmm", ctx.cur(pfx + "_qkmm"))
            ctx.wait("tensor", pfx + "_qfe", ctx.cur(pfx + "_qfe"))
            def qd_bc(e):
                e.matmul(ps_a[0][:, :], sb["ones1f"][:, :], strow[:, 1, :],
                         start=True, stop=True)
                return e.matmul(ps_a[1][:, :], sb["ones1f"][:, :], strow[:, 2, :],
                                start=True, stop=True)
            ctx.op("tensor", qd_bc, sem=pfx + "_qdb")
            ctx.wait("vector", pfx + "_qdb", ctx.cur(pfx + "_qdb"))
            def qdq_mk(e, hh=hl):
                e.tensor_mul(qdq[:, hh, 0:512], qf[:, 0:512], ps_a[0][:, :])
                return e.tensor_mul(qdq[:, hh, 512:1024], qf[:, 512:1024], ps_a[1][:, :])
            ctx.op("vector", qdq_mk, sem=pfx + "_qdq")

            for l in range(4 if _BCUT >= 3 else 0):
                msl = slice(l * 256, (l + 1) * 256)
                ctx.wait("tensor", pfx + "_prb", ctx.cur(pfx + "_prb"))
                ctx.wait("tensor", pfx + "_kte", ctx.cur(pfx + "_kte"))
                def sc_mm(e, ll=l):
                    r = None
                    for nc2 in range(2):
                        nchunk = ll * 256 + nc2 * 128
                        r = e.matmul(ps_sc[:, nc2, :], kf[:, nchunk:nchunk + 128],
                                     qf[:, ll * 256:(ll + 1) * 256], start=True, stop=True)
                    return r
                ctx.op("tensor", sc_mm, sem=pfx + "_scmm")
                ctx.wait("vector", pfx + "_scmm", ctx.cur(pfx + "_scmm"))
                ctx.wait("vector", pfx + "_dcm", ctx.cur(pfx + "_dcm"))
                ctx.wait("vector", pfx + "_omm", ctx.cur(pfx + "_omm"))
                ctx.op("vector", lambda e:
                       e.tensor_mul(prbB[:, :, :], ps_sc[:, :, :], decT[:, :, :]),
                       sem=pfx + "_prb")
                ctx.wait("tensor", pfx + "_prb", ctx.cur(pfx + "_prb"))
                ctx.wait("tensor", pfx + "_vte", ctx.cur(pfx + "_vte"))
                ctx.wait("tensor", pfx + "_oev", ctx.cur(pfx + "_oev"))
                ctx.wait("tensor", pfx + "_cora", ctx.cur(pfx + "_cora"))
                if l >= 1:
                    ctx.wait("tensor", pfx + "_kvbf", ctx.cur(pfx + "_kvbf"))
                if l == 0:
                    def o_mm(e, ll=l):
                        e.matmul(ps_o[:, :], vt[:, 0, :], prbB[:, 0, :],
                                 start=True, stop=False)
                        return e.matmul(ps_o[:, :], vt[:, 1, :], prbB[:, 1, :],
                                        start=False, stop=True)
                else:
                    def o_mm(e, ll=l, hh=hl):
                        for nc2 in range(2):
                            e.matmul(ps_o[:, :], vt[:, ll * 2 + nc2, :], prbB[:, nc2, :],
                                     start=(nc2 == 0), stop=False)
                        return e.matmul(ps_o[:, :], kvbf[:, :],
                                        qdq[:, hh, ll * 256:(ll + 1) * 256],
                                        start=False, stop=True)
                ctx.op("tensor", o_mm, sem=pfx + "_omm")
                ctx.wait("scalar", pfx + "_omm", ctx.cur(pfx + "_omm"))
                ctx.op("scalar", lambda e, hh=h, m=msl:
                       e.activation(ofb[:, hh, m], ps_o[:, :], AF.Copy), sem=pfx + "_oev")
                ctx.wait("scalar", pfx + "_kvmm", ctx.cur(pfx + "_kvmm"))
                def ktp_mk(e, ll=l):
                    for nc2 in range(2):
                        r = e.activation(ktp[:, nc2, :], ktm[:, ll * 2 + nc2, :], AF.Copy,
                                         scale=kde[:, nc2:nc2 + 1])
                    return r
                ctx.op("scalar", ktp_mk, sem=pfx + "_ktp")
                ctx.wait("tensor", pfx + "_ktp", ctx.cur(pfx + "_ktp"))
                ctx.wait("tensor", pfx + "_kvup", ctx.cur(pfx + "_kvup"))
                def kv_mm(e, ll=l):
                    e.matmul(ps_kv[:, :], ktp[:, 0, :], vt[:, ll * 2, :],
                             start=True, stop=False)
                    return e.matmul(ps_kv[:, :], ktp[:, 1, :], vt[:, ll * 2 + 1, :],
                                    start=False, stop=True)
                ctx.op("tensor", kv_mm, sem=pfx + "_kvmm")
                ctx.wait("vector", pfx + "_kvmm", ctx.cur(pfx + "_kvmm"))
                first_blk = (l == 0)
                def kv_up(e, hh=hl, f=first_blk, gh=h):
                    if f:
                        return e.tensor_copy(kvst[:, hh, :], ps_kv[:, :])
                    return e.scalar_tensor_tensor(kvst[:, hh, :], kvst[:, hh, :],
                                                  bdp[:, 1, gh:gh + 1], ps_kv[:, :],
                                                  ALU.mult, ALU.add)
                ctx.op("vector", kv_up, sem=pfx + "_kvup")
                if l < 3:
                    ctx.wait("scalar", pfx + "_kvup", ctx.cur(pfx + "_kvup"))
                    ctx.wait("scalar", pfx + "_cor_mm", ctx.cur(pfx + "_cor_mm"))
                    ctx.op("scalar", lambda e, hh=hl:
                           e.activation(kvbf[:, :], kvst[:, hh, :], AF.Copy),
                           sem=pfx + "_kvbf")

        n_kvup = ctx.cur(pfx + "_kvup")
        for p in range(4 if _BCUT >= 4 else 0):
            ctx.wait("vector", pfx + "_kvup", n_kvup)
            ctx.wait("vector", pfx + "_stgd", ctx.cur(pfx + "_stgd"))
            ctx.op("vector", lambda e, pp=p:
                   e.tensor_scalar_mul(kstg[:, :],
                                       kvst[:, :, :].rearrange("p h v -> p (h v)"),
                                       pmt[:, pp:pp + 1]), sem=pfx + "_stg")
            ctx.wait("sync", pfx + "_stg", ctx.cur(pfx + "_stg"))
            ctx.dma("sync", kvgd[hg].ap()[p:p + 1].squeeze(0), kstg[:, :], pfx + "_stgd")
        if _BCUT < 4:
            continue
        ctx.wait("gpsimd", pfx + "_stgd", ctx.cur(pfx + "_stgd"))
        def coll(e, g=hg):
            return e.collective_compute(
                "AllReduce", ALU.add,
                replica_groups=[[0, 1, 2, 3], [4, 5, 6, 7]],
                ins=[kvgd[g].ap()[:, :, :]],
                outs=[kvgd[g].ap()[:, :, :]],
                unique_tensors="No")
        ctx.op("gpsimd", coll, sem=pfx + "_cc2")
        ctx.wait("gpsimd", pfx + "_cc2", ctx.cur(pfx + "_cc2"))
        for p in range(4):
            ctx.wait("gpsimd", pfx + "_kvc", ctx.cur(pfx + "_kvc"))
            ctx.dma("gpsimd", kstg[:, :], kvgd[hg].ap()[p:p + 1].squeeze(0), pfx + "_kvr")
            ctx.wait("vector", pfx + "_kvr", ctx.cur(pfx + "_kvr"))
            ctx.wait("vector", pfx + "_corb", ctx.cur(pfx + "_corb"))
            def kv_comb(e, pp=p, g=hg):
                r = None
                for hh in range(HG):
                    c = ccoef[:, pp * 16 + g * HG + hh:pp * 16 + g * HG + hh + 1]
                    if pp == 0:
                        r = e.tensor_scalar_mul(kvin[:, hh, :],
                                                kstg[:, hh * 128:(hh + 1) * 128], c)
                    else:
                        r = e.scalar_tensor_tensor(kvin[:, hh, :],
                                                   kstg[:, hh * 128:(hh + 1) * 128], c,
                                                   kvin[:, hh, :], ALU.mult, ALU.add)
                return r
            ctx.op("vector", kv_comb, sem=pfx + "_kvc")

        n_kvc = ctx.cur(pfx + "_kvc")
        for hl in range(HG if _BCUT >= 5 else 0):
            h = hg * HG + hl
            for l in range(4):
                ctx.wait("scalar", pfx + "_kvc", n_kvc)
                ctx.wait("scalar", pfx + "_cor_mm", ctx.cur(pfx + "_cor_mm"))
                ctx.op("scalar", lambda e, hh=hl, gh=h, ll=l:
                       e.activation(kvbf[:, :], kvin[:, hh, :], AF.Copy,
                                    scale=bdp[:, ll, gh:gh + 1]), sem=pfx + "_corb")
                ctx.wait("tensor", pfx + "_corb", ctx.cur(pfx + "_corb"))
                ctx.wait("tensor", pfx + "_cora", ctx.cur(pfx + "_cora"))
                ctx.op("tensor", lambda e, hh=hl, ll=l:
                       e.matmul(ps_o[:, :], kvbf[:, :],
                                qdq[:, hh, ll * 256:(ll + 1) * 256],
                                start=True, stop=True), sem=pfx + "_cor_mm")
                ctx.wait("vector", pfx + "_cor_mm", ctx.cur(pfx + "_cor_mm"))
                ctx.op("vector", lambda e, gh=h, ll=l:
                       e.tensor_add(ofb[:, gh, ll * 256:(ll + 1) * 256],
                                    ofb[:, gh, ll * 256:(ll + 1) * 256], ps_o[:, :]),
                       sem=pfx + "_cora")
        if hg == 0:
            ctx.wait("vector", pfx + "_stgd", ctx.cur(pfx + "_stgd"))

    pps.close()
    qes.close()

    if "bof" in _DBG:
        t = dout("bof", (128, H, TQ), BF16)
        ctx.wait("sync", pfx + "_cora", ctx.cur(pfx + "_cora"))
        ctx.dma("sync", t.ap(), ofb[:, :, :], "dbg")
        ctx.wait("vector", "dbg", ctx.cur("dbg"))

    if _BCUT <= 5:
        fes.close()
        return

    oes = ExitStack()
    normw = oes.enter_context(nc.sbuf_tensor("bnw", [128, FC], F32))
    osq = oes.enter_context(nc.sbuf_tensor("bosq", [128, H, 512], BF16))
    gtmp = oes.enter_context(nc.sbuf_tensor("bgt", [128, 512], BF16))
    wgs = [oes.enter_context(nc.sbuf_tensor(f"bwg{i}", [128, FC, 128], BF16)) for i in range(2)]
    wos = [oes.enter_context(nc.sbuf_tensor(f"bwo{i}", [128, FC, 128], BF16)) for i in range(2)]
    rrow = oes.enter_context(nc.sbuf_tensor("brr", [1, 2, 512], F32))
    pps2 = ExitStack()
    ps_g = [pps2.enter_context(nc.psum_tensor(f"bpg{i}", [128, 512], F32)) for i in range(2)]
    ps_r = pps2.enter_context(nc.psum_tensor("bpr", [128, 512], F32))
    ps_y = [pps2.enter_context(nc.psum_tensor(f"bpy{i}", [128, 512], F32)) for i in range(2)]

    ctx.wait("gpsimd", pfx + "_cora", ctx.cur(pfx + "_cora"))
    ctx.wait("sync", pfx + "_cora", ctx.cur(pfx + "_cora"))
    ctx.dma("sync", normw[:, :], dr["normw"].ap(), pfx + "_nw")
    n_cora = ctx.cur(pfx + "_cora")

    for hf in range(2):
        tsl = slice(hf * 512, (hf + 1) * 512)
        ctx.wait("vector", pfx + "_cora", n_cora)
        ctx.wait("vector", pfx + "_rmm", ctx.cur(pfx + "_rmm"))
        ctx.op("vector", lambda e, t=tsl:
               e.tensor_mul(osq[:, :, :], ofb[:, :, t], ofb[:, :, t]), sem=pfx + "_osq")
        ctx.wait("tensor", pfx + "_osq", ctx.cur(pfx + "_osq"))
        ctx.wait("tensor", pfx + "_rr", ctx.cur(pfx + "_rr"))
        ctx.wait("tensor", pfx + "_gom", ctx.cur(pfx + "_gom"))
        def rms_mm(e):
            for hh in range(H):
                r = e.matmul(ps_r[0:1, :], sb["onesb"][:, :], osq[:, hh, :],
                             start=(hh == 0), stop=(hh == H - 1))
            return r
        ctx.op("tensor", rms_mm, sem=pfx + "_rmm")
        ctx.wait("vector", pfx + "_rmm", ctx.cur(pfx + "_rmm"))
        def rcalc(e, f=hf):
            e.tensor_scalar_mul(rrow[:, 0, :], ps_r[0:1, :], 1.0 / D)
            return e.tensor_scalar_add(rrow[:, 0, :], rrow[:, 0, :], 1e-6)
        ctx.op("vector", rcalc, sem=pfx + "_rc")
        ctx.wait("scalar", pfx + "_rc", ctx.cur(pfx + "_rc"))
        ctx.op("scalar", lambda e: e.activation(rrow[:, 0, :], rrow[:, 0, :], AF.Sqrt),
               sem=pfx + "_rs")
        ctx.wait("vector", pfx + "_rs", ctx.cur(pfx + "_rs"))
        ctx.op("vector", lambda e: e.reciprocal(rrow[:, 1, :], rrow[:, 0, :]),
               sem=pfx + "_rrec")
        ctx.wait("tensor", pfx + "_rrec", ctx.cur(pfx + "_rrec"))
        ctx.op("tensor", lambda e: e.matmul(ps_r[:, :], sb["ones1f"][:, :], rrow[:, 1, :],
                                            start=True, stop=True), sem=pfx + "_rr")

        for h in range(H):
            gb = h % 2
            if True:
                ctx.wait("gpsimd", pfx + "_gmm", max(0, ctx.cur(pfx + "_gmm") - 1))
                ctx.dma("gpsimd", wgs[gb][:, :, :],
                        dr["wgateT"].ap().rearrange("(c p) o -> p c o", p=128)
                        [:, :, h * 128:(h + 1) * 128], pfx + "_gw")
            ctx.wait("tensor", pfx + "_gw", ctx.cur(pfx + "_gw"))
            ctx.wait("tensor", pfx + "_gev", max(0, ctx.cur(pfx + "_gev") - 1))
            def g_mm(e, b=gb, t=tsl):
                r = None
                for kc in range(FC):
                    r = e.matmul(ps_g[b][:, :], wgs[b][:, kc, :], htile[:, kc, t],
                                 start=(kc == 0), stop=(kc == FC - 1))
                return r
            ctx.op("tensor", g_mm, sem=pfx + "_gmm")
            ctx.wait("scalar", pfx + "_gmm", ctx.cur(pfx + "_gmm"))
            ctx.wait("scalar", pfx + "_gof", ctx.cur(pfx + "_gof"))
            ctx.op("scalar", lambda e, b=gb:
                   e.activation(gtmp[:, :], ps_g[b][:, :], AF.Sigmoid), sem=pfx + "_gev")
            ctx.wait("vector", pfx + "_gev", ctx.cur(pfx + "_gev"))
            ctx.wait("vector", pfx + "_rr", ctx.cur(pfx + "_rr"))
            def go_mk(e, hh=h, t=tsl):
                e.tensor_mul(gtmp[:, :], gtmp[:, :], ofb[:, hh, t])
                return e.tensor_mul(gtmp[:, :], gtmp[:, :], ps_r[:, :])
            ctx.op("vector", go_mk, sem=pfx + "_gom")
            ctx.wait("scalar", pfx + "_gom", ctx.cur(pfx + "_gom"))
            ctx.op("scalar", lambda e, hh=h, t=tsl:
                   e.activation(ofb[:, hh, t], gtmp[:, :], AF.Copy,
                                scale=normw[:, hh:hh + 1]), sem=pfx + "_gof")

        for fc in range(FC):
            wb = fc % 2
            ctx.wait("gpsimd", pfx + "_ymm", max(0, ctx.cur(pfx + "_ymm") - 1))
            ctx.dma("gpsimd", wos[wb][:, :, :],
                    dr["woutT"].ap().rearrange("(c p) o -> p c o", p=128)
                    [:, :, fc * 128:(fc + 1) * 128], pfx + "_yw")
            ctx.wait("tensor", pfx + "_yw", ctx.cur(pfx + "_yw"))
            if fc == 0:
                ctx.wait("tensor", pfx + "_gof", ctx.cur(pfx + "_gof"))
            ctx.wait("tensor", pfx + "_yadd", max(0, ctx.cur(pfx + "_yadd") - 1))
            def y_mm(e, b=wb, t=tsl):
                r = None
                for oc in range(FC):
                    r = e.matmul(ps_y[b][:, :], wos[b][:, oc, :], ofb[:, oc, t],
                                 start=(oc == 0), stop=(oc == FC - 1))
                return r
            ctx.op("tensor", y_mm, sem=pfx + "_ymm")
            ctx.wait("vector", pfx + "_ymm", ctx.cur(pfx + "_ymm"))
            ctx.op("vector", lambda e, b=wb, f=fc, t=tsl:
                   e.tensor_add(resid[:, f, t], resid[:, f, t], ps_y[b][:, :]),
                   sem=pfx + "_yadd")

    ctx.wait("sync", pfx + "_yadd", ctx.cur(pfx + "_yadd"))
    ctx.op("sync", lambda e: e.nop(), sem="resid_ok")
    if "bres" in _DBG:
        t = dout("bres", (128, FC, TQ), F32)
        ctx.wait("sync", pfx + "_yadd", ctx.cur(pfx + "_yadd"))
        ctx.dma("sync", t.ap(), resid[:, :, :], "dbg")
    pps2.close()
    oes.close()
    fes.close()





def _assemble(res):
    out = np.empty((B, S, D), dtype=np.float32)
    for c in range(NCORE):
        b, q = c // 4, c % 4
        out[b, q * TQ:(q + 1) * TQ, :] = res.results[c]["yf"].T
    return out


_LAST_RESULTS = None


def kernel(**inputs):
    global _LAST_RESULTS
    in_maps = _host_prepare(inputs)
    nc = _build()
    res = run_bass_kernel_spmd(nc, in_maps, core_ids=list(range(NCORE)))
    _LAST_RESULTS = res
    return _assemble(res)


# version 03 (final)
# speedup vs baseline: 1.2203x; 1.2203x over previous
import os
from collections import defaultdict
from contextlib import ExitStack

import numpy as np

import concourse.bass as bass
import concourse.mybir as mybir
from concourse.bass_utils import run_bass_kernel_spmd

F32 = mybir.dt.float32
BF16 = mybir.dt.bfloat16
I8 = mybir.dt.int8
AF = mybir.ActivationFunctionType
ALU = mybir.AluOpType

B, S, D, H, HD, LAT, PD = 2, 4096, 2048, 16, 128, 512, 64
FF, LR = 8192, 16
TQ = 1024
FC = D // 128
NCORE = 8

_MAX_IMPL = 5
_STAGE = min(int(os.environ.get("MK_STAGE", "99")), _MAX_IMPL)
_NH = int(os.environ.get("MK_NH", "16"))
_PAR = int(os.environ.get("MK_PARANOID", "0"))
_BCUT = int(os.environ.get("MK_BCUT", "99"))
_PAR_A = bool(_PAR & 1)
_PAR_B = bool(_PAR & 2)
_PAR_C = bool(_PAR & 4)
_DBG = set(s for s in os.environ.get("MK_DBG", "").split(",") if s)


class Ctx:

    ENGINES = ("tensor", "vector", "scalar", "gpsimd", "sync")

    def __init__(self, nc, es):
        self.nc = nc
        self.es = es
        self.q = {e: [] for e in self.ENGINES}
        self.cnt = defaultdict(int)
        self.pools = {"d": [], "e": []}
        self.nalloc = {"d": 0, "e": 0}
        self.slot_of = {}
        self.waits = []

    POOL_CAP = {"d": 22, "e": 79}

    SETUP_CHAIN = {"b_neg", "b_nbc", "b_nsb", "b_bdr", "b_bde", "b_bdb",
                   "b_bdp", "b_pcr", "b_pce", "b_pcm", "b_ccb", "b_cc"}
    MERGE = {
        "b_osq": "b_rms", "b_rmm": "b_rms", "b_rc": "b_rms", "b_rs": "b_rms",
        "b_rrec": "b_rms", "b_rr": "b_rms",
        "b_ktp": "b_kv", "b_kvmm": "b_kv", "b_kvup": "b_kv", "b_kvbf": "b_kv",
        "b_corb": "b_cor", "b_cor_mm": "b_cor", "b_cora": "b_cor",
        "b_scmm": "b_sc", "b_prb": "b_sc",
        "b_omm": "b_oe", "b_oev": "b_oe",
        "b_qde": "b_qd", "b_qdb": "b_qd",
    }

    @staticmethod
    def _c(name):
        if name.startswith("f1_") or name.startswith("f2_"):
            return "f_" + name[3:]
        if name.startswith("bln0") or name.startswith("bln1"):
            name = "bln" + name[4:]
        if name in Ctx.SETUP_CHAIN:
            return "b_set"
        return Ctx.MERGE.get(name, name)

    def _slot(self, name, klass):
        name = self._c(name)
        if name in self.slot_of:
            sid, kl = self.slot_of[name]
            assert klass is None or kl == klass, f"sem {name} class {kl}->{klass}"
            return sid
        assert klass is not None, f"first touch of sem {name} without class"
        pool = self.pools[klass]
        idx = self.nalloc[klass]
        self.nalloc[klass] += 1
        assert idx < self.POOL_CAP[klass], f"semaphore pool {klass} exhausted at {name}"
        pool.append(self.es.enter_context(
            self.nc.semaphore(f"{klass}{len(pool)}")))
        sid = (klass, idx)
        self.slot_of[name] = (sid, klass)
        return sid

    def sem(self, name, klass=None):
        sid = self._slot(name, klass)
        return self.pools[sid[0]][sid[1]]

    def inc(self, name, n):
        sid = self._slot(name, "d" if n >= 16 else "e")
        self.cnt[sid] += n
        return self.cnt[sid]

    def cur(self, name):
        name = self._c(name)
        if name not in self.slot_of:
            return 0
        return self.cnt[self.slot_of[name][0]]

    def emit(self, eng, fn):
        self.q[eng].append(fn)

    def dma(self, eng, dst, src, sem):
        tgt = self.inc(sem, 16)
        s = self.sem(sem)
        self.emit(eng, lambda e, d=dst, i=src, s=s: e.dma_start(d, i).then_inc(s, 16))
        return tgt

    def wait(self, eng, sem, val=None):
        sem = self._c(sem)
        if val is None:
            val = self.cur(sem)
        if val <= 0:
            return
        s = self.sem(sem)
        self.waits.append((sem, val))
        self.emit(eng, lambda e, s=s, v=val: e.wait_ge(s, v))

    def op(self, eng, fn, sem=None, n=1):
        if sem is None:
            self.emit(eng, fn)
            return None
        tgt = self.inc(sem, n)
        s = self.sem(sem)
        self.emit(eng, lambda e, f=fn, s=s, n=n: f(e).then_inc(s, n))
        return tgt

    def check(self):
        for sem, val in self.waits:
            assert val <= self.cur(sem), (
                f"wait on {sem} for {val} but total emitted is {self.cur(sem)} (would hang)"
            )

    def finalize(self):
        self.check()
        with self.nc.Block() as blk:
            @blk.tensor
            def _(t):
                for f in self.q["tensor"]:
                    f(t)

            @blk.vector
            def _(v):
                for f in self.q["vector"]:
                    f(v)

            @blk.scalar
            def _(s):
                for f in self.q["scalar"]:
                    f(s)

            @blk.gpsimd
            def _(g):
                for f in self.q["gpsimd"]:
                    f(g)

            @blk.sync
            def _(sy):
                for f in self.q["sync"]:
                    f(sy)



def _host_prepare(inp):
    f32 = lambda a: np.ascontiguousarray(np.asarray(a, dtype=np.float32))
    x0 = f32(inp["L0_attn_a_hidden_states"])

    common = {}
    common["wdkv"] = f32(inp["L0_attn_a_W_DKV"])
    common["wdq"] = f32(inp["L0_attn_a_W_DQ"])
    common["wuk"] = f32(inp["L0_attn_a_W_UK"])
    common["wuv"] = f32(inp["L0_attn_a_W_UV"])
    common["wuq"] = f32(inp["L0_attn_a_W_UQ"])
    common["wkr"] = f32(inp["L0_attn_a_W_KR"])
    common["wqr"] = f32(inp["L0_attn_a_W_QR"])
    common["wo"] = f32(inp["L0_attn_a_W_O"])
    inv = (1.0 / 10000.0 ** (np.arange(0, PD, 2, dtype=np.float64) / PD))
    t = np.arange(S, dtype=np.float64)
    fr = t[:, None] * inv[None, :]
    emb = np.concatenate([fr, fr], axis=1)
    common["rcos"] = np.ascontiguousarray(np.cos(emb).T.astype(np.float32))
    sgn = np.concatenate([-np.ones(32), np.ones(32)])[:, None]
    common["rsin"] = np.ascontiguousarray((np.sin(emb).T * sgn).astype(np.float32))

    wupT = np.asarray(inp["L0_ffn_w_up_q"]).T.astype(np.int8)
    common["wupT"] = np.ascontiguousarray(
        wupT.reshape(16, 128, 32, 256).transpose(2, 1, 0, 3))
    common["supT"] = np.ascontiguousarray(f32(inp["L0_ffn_w_up_scale"]).T)
    common["bup"] = np.ascontiguousarray(f32(inp["L0_ffn_b_up"]).reshape(FF // 128, 128).T)
    common["aup"] = f32(inp["L0_ffn_w_up_lora_a"])
    common["bupl"] = f32(inp["L0_ffn_w_up_lora_b"])
    wdnT = np.asarray(inp["L0_ffn_w_down_q"]).T.astype(np.int8)
    common["wdnT"] = np.ascontiguousarray(
        wdnT.reshape(64, 128, 16, 128).transpose(2, 1, 0, 3))
    common["sdnT"] = np.ascontiguousarray(f32(inp["L0_ffn_w_down_scale"]).T)
    common["bdn"] = np.ascontiguousarray(f32(inp["L0_ffn_b_down"]).reshape(FC, 128).T)
    common["adn"] = f32(inp["L0_ffn_w_down_lora_a"])
    common["bdnl"] = f32(inp["L0_ffn_w_down_lora_b"])

    common["slope"] = np.ascontiguousarray(f32(inp["L0_attn_b_slope_rate"]).reshape(1, H))
    common["wqkvT"] = np.ascontiguousarray(f32(inp["L0_attn_b_w_qkv"]).T)
    common["wgateT"] = np.ascontiguousarray(f32(inp["L0_attn_b_w_gate"]).T)
    common["woutT"] = np.ascontiguousarray(f32(inp["L0_attn_b_w_out"]).T)
    common["normw"] = np.ascontiguousarray(f32(inp["L0_attn_b_norm_w"]).reshape(FC, 128).T)
    common["qposg"] = np.arange(1, TQ + 1, dtype=np.float32).reshape(1, TQ)
    common["qposl"] = ((np.arange(TQ) % 256) + 1).astype(np.float32).reshape(1, TQ)
    common["kdecg"] = np.ascontiguousarray(
        (255 - np.arange(256)).astype(np.float32).reshape(2, 128).T)
    n_ = np.arange(256)
    m_ = np.arange(256)
    common["dgexp"] = np.ascontiguousarray(
        np.maximum(m_[None, :] - n_[:, None], 0).astype(np.float32).reshape(2, 128, 256))
    common["dgmask"] = np.ascontiguousarray(
        (m_[None, :] >= n_[:, None]).astype(np.float32).reshape(2, 128, 256))
    common["ident"] = np.eye(128, dtype=np.float32)

    in_maps = []
    for c in range(NCORE):
        b, q = c // 4, c % 4
        m = dict(common)
        x0t = np.ascontiguousarray(x0[b].T)
        m["x0f"] = x0t
        m["x0own"] = np.ascontiguousarray(x0t[:, q * TQ:(q + 1) * TQ])
        m["rcown"] = np.ascontiguousarray(common["rcos"][:, q * TQ:(q + 1) * TQ])
        m["rsown"] = np.ascontiguousarray(common["rsin"][:, q * TQ:(q + 1) * TQ])
        pm = np.zeros((128, 4), dtype=np.float32)
        pm[:, q] = 1.0
        m["pmask"] = pm
        pg = np.zeros((1, 4), dtype=np.float32)
        pc = np.zeros((1, 4), dtype=np.float32)
        for p in range(4):
            if p < q:
                pg[0, p] = 1024.0 * (q - p - 1)
                pc[0, p] = 1.0
        m["pcoefg"] = pg
        m["pcmask"] = pc
        in_maps.append(m)
    return in_maps



def _build():
    nc = bass.Bass("TRN2", target_bir_lowering=False, debug=False, num_devices=8)
    es = ExitStack()
    ctx = Ctx(nc, es)

    dr = {}
    def din(name, shape, dtype=F32):
        dr[name] = nc.dram_tensor(name, list(shape), dtype, kind="ExternalInput")

    din("x0f", (D, S)); din("x0own", (D, TQ))
    for n, sh in [("wdkv", (D, LAT)), ("wdq", (D, LAT)), ("wuk", (LAT, D)),
                  ("wuv", (LAT, D)), ("wuq", (LAT, D)), ("wkr", (D, PD)),
                  ("wqr", (LAT, H * PD)), ("wo", (D, D)),
                  ("rcos", (PD, S)), ("rsin", (PD, S))]:
        din(n, sh)
    din("wupT", (32, 128, 16, 256), I8); din("supT", (32, FF)); din("bup", (128, FF // 128))
    din("aup", (D, LR)); din("bupl", (LR, FF))
    din("wdnT", (16, 128, 64, 128), I8); din("sdnT", (128, D)); din("bdn", (128, FC))
    din("adn", (FF, LR)); din("bdnl", (LR, D))
    din("slope", (1, H)); din("wqkvT", (D, 3 * H * HD)); din("wgateT", (D, D))
    din("woutT", (D, D)); din("normw", (128, FC))
    din("qposg", (1, TQ)); din("qposl", (1, TQ)); din("kdecg", (128, 2))
    din("dgexp", (2, 128, 256)); din("dgmask", (2, 128, 256))
    din("ident", (128, 128)); din("pmask", (128, 4)); din("pcoefg", (1, 4))
    din("pcmask", (1, 4)); din("rcown", (PD, TQ)); din("rsown", (PD, TQ))

    yf = nc.dram_tensor("yf", [D, TQ], F32, kind="ExternalOutput")

    dbg = {}
    def dout(name, shape, dtype=F32):
        if name in _DBG:
            dbg[name] = nc.dram_tensor("dbg_" + name, list(shape), dtype,
                                       kind="ExternalOutput")
        return dbg.get(name)

    sb = {}
    def salloc(stack, name, shape, dtype):
        sb[name] = stack.enter_context(nc.sbuf_tensor(name, list(shape), dtype))
        return sb[name]

    salloc(es, "resid", [128, FC, TQ], F32)
    salloc(es, "onesb", [128, 1], BF16)
    salloc(es, "onesf", [128, 1], F32)
    salloc(es, "ones1f", [1, 128], F32)
    salloc(es, "identt", [128, 128], BF16)
    salloc(es, "nrow", [1, 16], F32)
    salloc(es, "brow", [1, 16], F32)
    salloc(es, "drow", [1, 16], F32)

    ctx.op("gpsimd", lambda e: e.memset(sb["onesb"][:, :], 1.0))
    ctx.op("gpsimd", lambda e: e.memset(sb["onesf"][:, :], 1.0))
    ctx.op("gpsimd", lambda e: e.memset(sb["ones1f"][:, :], 1.0), sem="const")
    ctx.dma("gpsimd", sb["identt"][:, :], dr["ident"].ap(), "const_ld")
    ctx.dma("sync", sb["resid"][:, :, :],
            dr["x0own"].ap().rearrange("(c p) t -> p c t", p=128), "resid_ld")
    ctx.inc("resid_ok", 0)

    if _STAGE >= 1:
        _p_attn_a(ctx, nc, es, dr, sb, dout, salloc)
    if _STAGE >= 3:
        _p_ffn(ctx, nc, es, dr, sb, dout, salloc, tag="f1")
    if _STAGE >= 4:
        _p_attn_b(ctx, nc, es, dr, sb, dout, salloc)
    if _STAGE >= 5:
        _p_ffn(ctx, nc, es, dr, sb, dout, salloc, tag="f2")

    ctx.wait("sync", "resid_ld")
    ctx.wait("sync", "resid_ok")
    ctx.dma("sync", yf.ap().rearrange("(c p) t -> p c t", p=128), sb["resid"][:, :, :],
            "store")
    ctx.wait("sync", "store")
    ctx.wait("sync", "dbg")

    ctx.finalize()
    es.close()
    return nc



def _p_attn_a(ctx, nc, es, dr, sb, dout, salloc):
    SL = 256
    NSL = S // SL

    aes = ExitStack()
    salloc(aes, "ckv", [128, 4, S], BF16)
    salloc(aes, "krr", [64, S], BF16)
    salloc(aes, "cq", [128, 4, TQ], BF16)

    ckv, krr, cq, resid = sb["ckv"], sb["krr"], sb["cq"], sb["resid"]

    p1 = ExitStack()
    xsl = [p1.enter_context(nc.sbuf_tensor(f"xsl{i}", [128, FC, SL], F32)) for i in range(2)]
    xb = p1.enter_context(nc.sbuf_tensor("xb", [128, FC, SL], BF16))
    hsl = [p1.enter_context(nc.sbuf_tensor(f"hsl{i}", [128, FC, SL], BF16)) for i in range(2)]
    wproj = p1.enter_context(nc.sbuf_tensor("wproj", [128, FC, LAT], BF16))
    wkr = p1.enter_context(nc.sbuf_tensor("wkrS", [128, FC, PD], BF16))
    strow = p1.enter_context(nc.sbuf_tensor("strow", [1, 2, 4, SL], F32))

    pps = ExitStack()
    ckv_ps = [pps.enter_context(nc.psum_tensor(f"ckvps{i}", [128, 4, SL], F32))
              for i in range(2)]
    st_ps = [pps.enter_context(nc.psum_tensor(f"stps{i}", [1, 2, SL], F32)) for i in range(2)]
    kr_ps = pps.enter_context(nc.psum_tensor("krps", [64, SL], F32))
    bc_ps = pps.enter_context(nc.psum_tensor("bcps", [128, 2, SL], F32))

    ctx.dma("gpsimd", wproj[:, :, :], dr["wdkv"].ap().rearrange("(c p) l -> p c l", p=128),
            "aw")
    ctx.dma("gpsimd", wkr[:, :, :], dr["wkr"].ap().rearrange("(c p) l -> p c l", p=128),
            "aw")
    w_ld = ctx.cur("aw")

    def sweep_slice(j, src_is_resid, proj_dst, proj_ps_sem, proj_evac_sem, do_kr):
        bi = j % 2
        tsl = slice((j % NSL) * SL, ((j % NSL) + 1) * SL) if not src_is_resid else \
            slice((j - NSL) * SL, (j - NSL + 1) * SL)

        if not src_is_resid:
            ctx.wait("sync", "a_h", max(0, j - 1))
            ctx.dma("sync", xsl[bi][:, :, :],
                    dr["x0f"].ap().rearrange("(c p) t -> p c t", p=128)[:, :, tsl], "a_x")
        ctx.wait("scalar", "a_stats_mm", max(0, j))
        if not src_is_resid:
            ctx.wait("scalar", "a_x", 16 * (j + 1))
        else:
            ctx.wait("scalar", "resid_ld")
        src = xsl[bi][:, :, :] if not src_is_resid else resid[:, :, tsl]
        ctx.op("scalar", lambda e, s=src: e.activation(xb[:, :, :], s, AF.Copy),
               sem="a_cast")
        ctx.wait("vector", "a_cast", j + 1)
        ctx.wait("vector", "a_proj_mm", max(0, j - 1))
        ctx.op("vector", lambda e, b=bi:
               e.tensor_mul(hsl[b][:, :, :], xb[:, :, :], xb[:, :, :]), sem="a_sq")
        ctx.wait("tensor", "a_sq", j + 1)
        ctx.wait("tensor", "a_statcalc", max(0, j - 1))
        def stats_mm(e, b=bi):
            for kc in range(FC):
                e.matmul(st_ps[b][:, 0, :], sb["onesb"][:, :], xb[:, kc, :],
                         start=(kc == 0), stop=(kc == FC - 1))
            for kc in range(FC):
                r = e.matmul(st_ps[b][:, 1, :], sb["onesb"][:, :], hsl[b][:, kc, :],
                             start=(kc == 0), stop=(kc == FC - 1))
            return r
        ctx.op("tensor", stats_mm, sem="a_stats_mm")
        ctx.wait("vector", "a_stats_mm", j + 1)
        def statcalc(e, b=bi):
            e.tensor_scalar_mul(strow[:, b, 0, :], st_ps[b][:, 0, :], 1.0 / D)
            e.tensor_scalar_mul(strow[:, b, 2, :], st_ps[b][:, 1, :], 1.0 / D)
            e.tensor_mul(strow[:, b, 3, :], strow[:, b, 0, :], strow[:, b, 0, :])
            e.tensor_sub(strow[:, b, 2, :], strow[:, b, 2, :], strow[:, b, 3, :])
            return e.tensor_scalar_add(strow[:, b, 2, :], strow[:, b, 2, :], 1e-5)
        ctx.op("vector", statcalc, sem="a_statcalc")
        ctx.wait("scalar", "a_statcalc", j + 1)
        ctx.op("scalar", lambda e, b=bi:
               e.activation(strow[:, b, 2, :], strow[:, b, 2, :], AF.Sqrt),
               sem="a_sd")
        ctx.wait("vector", "a_sd", j + 1)
        ctx.op("vector", lambda e, b=bi:
               e.reciprocal(strow[:, b, 1, :], strow[:, b, 2, :]), sem="a_rinv")
        ctx.wait("tensor", "a_rinv", j + 1)
        ctx.wait("tensor", "a_h", max(0, j))
        def bcast(e, b=bi):
            e.matmul(bc_ps[:, 0, :], sb["ones1f"][:, :], strow[:, b, 0, :],
                     start=True, stop=True)
            return e.matmul(bc_ps[:, 1, :], sb["ones1f"][:, :], strow[:, b, 1, :],
                            start=True, stop=True)
        ctx.op("tensor", bcast, sem="a_bc")
        ctx.wait("vector", "a_bc", j + 1)
        def happly(e, b=bi, s=src):
            mb = bc_ps[:, 0:1, :].broadcast_to([128, FC, SL])
            rb = bc_ps[:, 1:2, :].broadcast_to([128, FC, SL])
            e.tensor_sub(xsl[b][:, :, :], s, mb)
            return e.tensor_mul(hsl[b][:, :, :], xsl[b][:, :, :], rb)
        ctx.op("vector", happly, sem="a_h")
        ctx.wait("tensor", "a_h", j + 1)
        if j == 0:
            ctx.wait("tensor", "aw", w_ld)
        if j == NSL:
            ctx.wait("tensor", "aw", ctx.cur("aw"))
        ctx.wait("tensor", proj_evac_sem, max(0, ctx.cur(proj_ps_sem) - 1))
        if do_kr:
            ctx.wait("tensor", "a_kr_evac", ctx.cur("a_kr_evac"))
        def proj_mm(e, b=bi):
            for lc in range(4):
                for kc in range(FC):
                    r = e.matmul(ckv_ps[b][:, lc, :],
                                 wproj[:, kc, lc * 128:(lc + 1) * 128], hsl[b][:, kc, :],
                                 start=(kc == 0), stop=(kc == FC - 1))
            if do_kr:
                for kc in range(FC):
                    r = e.matmul(kr_ps[:, :], wkr[:, kc, :], hsl[b][:, kc, :],
                                 start=(kc == 0), stop=(kc == FC - 1))
            return r
        ctx.op("tensor", proj_mm, sem=proj_ps_sem)
        ctx.op("tensor", lambda e: e.nop(), sem="a_proj_mm")
        ctx.wait("scalar", proj_ps_sem, ctx.cur(proj_ps_sem))
        if do_kr:
            ctx.op("scalar", lambda e, t=tsl:
                   e.activation(krr[0:64, t], kr_ps[:, :], AF.Copy), sem="a_kr_evac")
        ctx.op("scalar", lambda e, b=bi, t=tsl, d=proj_dst:
               e.activation(d[:, :, t], ckv_ps[b][:, :, :], AF.Copy), sem=proj_evac_sem)

    for j in range(NSL):
        sweep_slice(j, False, ckv, "a_ckv_mm", "a_ckv_evac", True)

    ctx.wait("gpsimd", "a_proj_mm", NSL)
    ctx.dma("gpsimd", wproj[:, :, :], dr["wdq"].ap().rearrange("(c p) l -> p c l", p=128),
            "aw")

    ctx.wait("tensor", "a_ckv_evac", ctx.cur("a_ckv_evac"))
    ctx.wait("tensor", "a_kr_evac", ctx.cur("a_kr_evac"))
    for j in range(NSL, NSL + TQ // SL):
        sweep_slice(j, True, cq, "a_cq_mm", "a_cq_evac", False)

    n_cq = ctx.cur("a_cq_evac")
    n_ckv = ctx.cur("a_ckv_evac")
    n_kr = ctx.cur("a_kr_evac")

    ctx.wait("sync", "a_ckv_evac", n_ckv)
    ctx.wait("sync", "a_cq_evac", n_cq)
    pps.close()
    p1.close()

    p2 = ExitStack()
    rc = p2.enter_context(nc.sbuf_tensor("rcS", [64, S], F32))
    rs = p2.enter_context(nc.sbuf_tensor("rsS", [64, S], F32))
    krtmp = p2.enter_context(nc.sbuf_tensor("krtmp", [64, S], BF16))
    ctx.dma("sync", rc[:, :], dr["rcos"].ap(), "a_rope_ld")
    ctx.dma("sync", rs[:, :], dr["rsin"].ap(), "a_rope_ld")
    ctx.wait("sync", "a_kr_evac", n_kr)
    ctx.dma("sync", krtmp[0:32, :], krr[32:64, :], "a_rope_ld")
    ctx.dma("sync", krtmp[32:64, :], krr[0:32, :], "a_rope_ld")
    ctx.wait("vector", "a_rope_ld")
    ctx.wait("vector", "a_kr_evac", n_kr)
    def rope_kr(e):
        e.tensor_mul(krr[0:64, :], krr[0:64, :], rc[0:64, :])
        e.tensor_mul(krtmp[0:64, :], krtmp[0:64, :], rs[0:64, :])
        return e.tensor_add(krr[0:64, :], krr[0:64, :], krtmp[0:64, :])
    ctx.op("vector", rope_kr, sem="a_rope")
    ctx.wait("sync", "a_rope", 1)
    p2.close()

    if "ckv" in _DBG:
        t = dout("ckv", (128, 4, S), BF16)
        ctx.wait("sync", "a_ckv_evac", n_ckv)
        ctx.dma("sync", t.ap(), ckv[:, :, :], "dbg")
    if "krr" in _DBG:
        t = dout("krr", (64, S), BF16)
        ctx.wait("sync", "a_rope", 1)
        ctx.dma("sync", t.ap(), krr[:, :], "dbg")
    if "cq" in _DBG:
        t = dout("cq", (128, 4, TQ), BF16)
        ctx.wait("sync", "a_cq_evac", n_cq)
        ctx.dma("sync", t.ap(), cq[:, :, :], "dbg")

    if _STAGE >= 2:
        bes = ExitStack()
        salloc(bes, "of", [128, H, TQ], BF16)
        _p_attn_a_heads(ctx, nc, es, dr, sb, dout, salloc)
        bes.close()

    aes.close()


def _p_attn_a_heads(ctx, nc, es, dr, sb, dout, salloc):
    resid, ckv, krr, cq, of = sb["resid"], sb["ckv"], sb["krr"], sb["cq"], sb["of"]
    NKB = 8
    KB = 512
    SCL = 1.0 / float(np.sqrt(HD + PD))

    pps = ExitStack()
    o_ps = pps.enter_context(nc.psum_tensor("o_ps", [128, TQ], F32))
    sc_ps = [pps.enter_context(nc.psum_tensor(f"sc_ps{i}", [128, TQ], F32)) for i in range(2)]
    kc_ps = pps.enter_context(nc.psum_tensor("kc_ps", [128, KB], F32))
    vt_ps = pps.enter_context(nc.psum_tensor("vt_ps", [128, 4, 128], F32))

    hes = ExitStack()
    wukh = [hes.enter_context(nc.sbuf_tensor(f"wukh{i}", [128, 4, 128], BF16)) for i in range(2)]
    wuvh = [hes.enter_context(nc.sbuf_tensor(f"wuvh{i}", [128, 4, 128], BF16)) for i in range(2)]
    wuqh = [hes.enter_context(nc.sbuf_tensor(f"wuqh{i}", [128, 4, 128], BF16)) for i in range(2)]
    wqrh = [hes.enter_context(nc.sbuf_tensor(f"wqrh{i}", [128, 4, PD], BF16)) for i in range(2)]
    qch = hes.enter_context(nc.sbuf_tensor("qch", [128, TQ], BF16))
    qrh = hes.enter_context(nc.sbuf_tensor("qrh", [64, TQ], BF16))
    qrsw = hes.enter_context(nc.sbuf_tensor("qrsw", [64, TQ], BF16))
    rcown = hes.enter_context(nc.sbuf_tensor("rcownS", [64, TQ], F32))
    rsown = hes.enter_context(nc.sbuf_tensor("rsownS", [64, TQ], F32))
    kcb = [hes.enter_context(nc.sbuf_tensor(f"kcb{i}", [128, KB], BF16)) for i in range(2)]
    vtm = [hes.enter_context(nc.sbuf_tensor(f"vtm{i}", [128, 4, 128], BF16)) for i in range(2)]
    prb = [hes.enter_context(nc.sbuf_tensor(f"prb{i}", [128, TQ], BF16)) for i in range(2)]
    dacc = hes.enter_context(nc.sbuf_tensor("dacc", [128, TQ], F32))
    dinv = hes.enter_context(nc.sbuf_tensor("dinv", [128, TQ], F32))

    ctx.wait("gpsimd", "a_rope", 1)
    ctx.wait("sync", "a_rope", 1)
    ctx.wait("gpsimd", "a_cq_evac", ctx.cur("a_cq_evac"))
    ctx.wait("sync", "a_cq_evac", ctx.cur("a_cq_evac"))
    ctx.dma("sync", rcown[:, :], dr["rcown"].ap(), "h_rope_ld")
    ctx.dma("sync", rsown[:, :], dr["rsown"].ap(), "h_rope_ld")

    def head_weights(h, bi):
        hs = slice(h * 128, (h + 1) * 128)
        ctx.dma("gpsimd", wukh[bi][:, :, :],
                dr["wuk"].ap().rearrange("(lc p) d -> p lc d", p=128)[:, :, hs], "h_w")
        ctx.dma("gpsimd", wuvh[bi][:, :, :],
                dr["wuv"].ap().rearrange("(lc p) d -> p lc d", p=128)[:, :, hs], "h_w")
        ctx.dma("gpsimd", wuqh[bi][:, :, :],
                dr["wuq"].ap().rearrange("(lc p) d -> p lc d", p=128)[:, :, hs], "h_w")
        ctx.dma("gpsimd", wqrh[bi][:, :, :],
                dr["wqr"].ap().rearrange("(lc p) d -> p lc d", p=128)
                [:, :, h * PD:(h + 1) * PD], "h_w")

    head_weights(0, 0)

    for h in range(_NH):
        bi = h % 2
        if h + 1 < _NH:
            if h >= 1:
                ctx.wait("gpsimd", "h_kcmm", h * NKB)
                ctx.wait("gpsimd", "h_vtmm", h * NKB)
                ctx.wait("gpsimd", "h_qprep_mm", h)
            head_weights(h + 1, 1 - bi)

        ctx.wait("tensor", "h_w", 64 * (h + 1))
        ctx.wait("tensor", "a_cq_evac")
        if h >= 1:
            ctx.wait("tensor", "h_exp", h * NKB * 4)
            ctx.wait("tensor", "h_div", h)
        def qprep_mm(e, b=bi):
            for half in range(2):
                n = slice(half * 512, (half + 1) * 512)
                for lc in range(4):
                    e.matmul(sc_ps[0][:, n], wuqh[b][:, lc, :], cq[:, lc, n],
                             start=(lc == 0), stop=(lc == 3))
                for lc in range(4):
                    r = e.matmul(sc_ps[1][0:64, n], wqrh[b][:, lc, :], cq[:, lc, n],
                                 start=(lc == 0), stop=(lc == 3))
            return r
        ctx.op("tensor", qprep_mm, sem="h_qprep_mm")
        ctx.wait("vector", "h_qprep_mm", h + 1)
        def qprep_ev(e):
            e.tensor_scalar_mul(qch[:, :], sc_ps[0][:, :], SCL)
            return e.tensor_scalar_mul(qrh[:, :], sc_ps[1][0:64, :], SCL)
        ctx.op("vector", qprep_ev, sem="h_qprep_ev")
        ctx.wait("sync", "h_qprep_ev", h + 1)
        ctx.dma("sync", qrsw[0:32, :], qrh[32:64, :], "h_qsw")
        ctx.dma("sync", qrsw[32:64, :], qrh[0:32, :], "h_qsw")
        ctx.wait("vector", "h_qsw", 32 * (h + 1))
        if h == 0:
            ctx.wait("vector", "h_rope_ld")
        def qrope(e):
            e.tensor_mul(qrh[:, :], qrh[:, :], rcown[:, :])
            e.tensor_mul(qrsw[:, :], qrsw[:, :], rsown[:, :])
            return e.tensor_add(qrh[:, :], qrh[:, :], qrsw[:, :])
        ctx.op("vector", qrope, sem="h_qrope")

        for blk in range(NKB):
            ksl = slice(blk * KB, (blk + 1) * KB)
            ctx.wait("tensor", "h_kc_evac", ctx.cur("h_kc_evac"))
            def kc_mm(e, b=bi, k=ksl):
                for lc in range(4):
                    r = e.matmul(kc_ps[:, :], wukh[b][:, lc, :], ckv[:, lc, k],
                                 start=(lc == 0), stop=(lc == 3))
                return r
            ctx.op("tensor", kc_mm, sem="h_kcmm")
            ctx.wait("scalar", "h_kcmm", ctx.cur("h_kcmm"))
            ctx.wait("scalar", "h_scmm", max(0, ctx.cur("h_scmm") - 4))
            ctx.op("scalar", lambda e, b2=blk % 2: e.activation(kcb[b2][:, :], kc_ps[:, :], AF.Copy),
                   sem="h_kc_evac")
            ctx.wait("tensor", "h_vt_evac", ctx.cur("h_vt_evac"))
            def vt_mm(e, b=bi, k=ksl):
                for sub in range(4):
                    for lc in range(4):
                        r = e.matmul(vt_ps[:, sub, :],
                                     ckv[:, lc, k][:, sub * 128:(sub + 1) * 128],
                                     wuvh[b][:, lc, :],
                                     start=(lc == 0), stop=(lc == 3))
                return r
            ctx.op("tensor", vt_mm, sem="h_vtmm")
            ctx.wait("scalar", "h_vtmm", ctx.cur("h_vtmm"))
            ctx.wait("scalar", "h_avmm", max(0, ctx.cur("h_avmm") - 4))
            ctx.op("scalar", lambda e, b2=blk % 2: e.activation(vtm[b2][:, :, :], vt_ps[:, :, :], AF.Copy),
                   sem="h_vt_evac")

            for ktc in range(4):
                pi = ktc % 2
                kq = slice(blk * KB + ktc * 128, blk * KB + (ktc + 1) * 128)
                ctx.wait("tensor", "h_kc_evac", ctx.cur("h_kc_evac"))
                ctx.wait("tensor", "h_exp", max(0, ctx.cur("h_exp") - 1))
                def sc_mm(e, p=pi, kk=ktc, k=kq, b2=blk % 2):
                    for half in range(2):
                        n = slice(half * 512, (half + 1) * 512)
                        e.matmul(sc_ps[p][:, n], kcb[b2][:, kk * 128:(kk + 1) * 128],
                                 qch[:, n], start=True, stop=False)
                        r = e.matmul(sc_ps[p][:, n], krr[0:64, k], qrh[:, n],
                                     start=False, stop=True)
                    return r
                ctx.op("tensor", sc_mm, sem="h_scmm")
                ctx.wait("scalar", "h_scmm", ctx.cur("h_scmm"))
                ctx.wait("scalar", "h_dadd", max(0, ctx.cur("h_dadd") - 1))
                ctx.wait("scalar", "h_avmm", max(0, ctx.cur("h_avmm") - 1))
                ctx.op("scalar", lambda e, p=pi:
                       e.activation(prb[p][:, :], sc_ps[p][:, :], AF.Exp),
                       sem="h_exp")
                ctx.wait("vector", "h_exp", ctx.cur("h_exp"))
                first = (blk == 0 and ktc == 0)
                if first:
                    ctx.op("vector", lambda e, p=pi:
                           e.tensor_copy(dacc[:, :], prb[p][:, :]), sem="h_dadd")
                else:
                    ctx.op("vector", lambda e, p=pi:
                           e.tensor_add(dacc[:, :], dacc[:, :], prb[p][:, :]), sem="h_dadd")
                ctx.wait("tensor", "h_exp", ctx.cur("h_exp"))
                ctx.wait("tensor", "h_vt_evac", ctx.cur("h_vt_evac"))
                last = (blk == NKB - 1 and ktc == 3)
                def av_mm(e, p=pi, kk=ktc, f=first, l=last, b2=blk % 2):
                    for half in range(2):
                        n = slice(half * 512, (half + 1) * 512)
                        r = e.matmul(o_ps[:, n], vtm[b2][:, kk, :], prb[p][:, n],
                                     start=f, stop=l)
                    return r
                ctx.op("tensor", av_mm, sem="h_avmm")

        ctx.wait("tensor", "h_dadd", ctx.cur("h_dadd"))
        def d_mm(e):
            for half in range(2):
                n = slice(half * 512, (half + 1) * 512)
                r = e.matmul(sc_ps[0][0:1, n], sb["onesf"][:, :], dacc[:, n],
                             start=True, stop=True)
            return r
        ctx.op("tensor", d_mm, sem="h_dmm")
        ctx.wait("vector", "h_dmm", h + 1)
        ctx.op("vector", lambda e:
               e.reciprocal(dacc[0:1, 0:TQ], sc_ps[0][0:1, 0:TQ]), sem="h_drec")
        ctx.wait("tensor", "h_drec", h + 1)
        def dbc_mm(e):
            for half in range(2):
                n = slice(half * 512, (half + 1) * 512)
                r = e.matmul(sc_ps[1][:, n], sb["ones1f"][:, :], dacc[0:1, n],
                             start=True, stop=True)
            return r
        ctx.op("tensor", dbc_mm, sem="h_dbc")
        ctx.wait("vector", "h_dbc", h + 1)
        ctx.op("vector", lambda e: e.tensor_copy(dinv[:, :], sc_ps[1][:, :]), sem="h_dcp")
        ctx.wait("vector", "h_avmm", ctx.cur("h_avmm"))
        ctx.op("vector", lambda e, hh=h:
               e.tensor_mul(of[:, hh, :], o_ps[:, :], dinv[:, :]), sem="h_div")
        ctx.op("vector", lambda e, hh=h:
               e.tensor_copy(sb["drow"][0:1, hh:hh + 1], dinv[0:1, 0:1]), sem="h_drec2")

    n_div = ctx.cur("h_div")
    woes = ExitStack()
    wob = [woes.enter_context(nc.sbuf_tensor(f"wob{i}", [128, FC, 128], BF16))
           for i in range(3)]
    for fc in range(FC):
        bi = fc % 3
        ctx.wait("gpsimd", "h_wo_mm", max(0, fc - 2))
        ctx.dma("gpsimd", wob[bi][:, :, :],
                dr["wo"].ap().rearrange("(dc p) f -> p dc f", p=128)
                [:, :, fc * 128:(fc + 1) * 128], "h_wo_ld")
        ctx.wait("tensor", "h_wo_ld", 16 * (fc + 1))
        if fc == 0:
            ctx.wait("tensor", "h_div", n_div)
        ctx.wait("tensor", "h_wo_add", max(0, fc - 1))
        pb2 = fc % 2
        def wo_mm(e, b=bi, p=pb2):
            for half in range(2):
                n = slice(half * 512, (half + 1) * 512)
                for dc in range(FC):
                    r = e.matmul(sc_ps[p][:, n], wob[b][:, dc, :], of[:, dc, n],
                                 start=(dc == 0), stop=(dc == FC - 1))
            return r
        ctx.op("tensor", wo_mm, sem="h_wo_mm")
        ctx.wait("vector", "h_wo_mm", fc + 1)
        ctx.op("vector", lambda e, p=pb2, f=fc:
               e.tensor_add(resid[:, f, :], resid[:, f, :], sc_ps[p][:, :]),
               sem="h_wo_add")
    woes.close()
    ctx.wait("sync", "h_wo_add", ctx.cur("h_wo_add"))
    ctx.op("sync", lambda e: e.nop(), sem="resid_ok")

    if "x1" in _DBG:
        t = dout("x1", (128, FC, TQ), F32)
        ctx.wait("sync", "h_wo_add", ctx.cur("h_wo_add"))
        ctx.dma("sync", t.ap(), resid[:, :, :], "dbg")
    for nm, tile, shp, dt_, sem in [
        ("of", of[:, :, :], (128, H, TQ), BF16, "h_div"),
        ("drow", sb["drow"][:, :], (1, 16), F32, "h_drec2"),
        ("qch", qch[:, :], (128, TQ), BF16, "h_div"),
        ("qrh", qrh[:, :], (64, TQ), BF16, "h_div"),
        ("dinv", dinv[:, :], (128, TQ), F32, "h_div"),
        ("dacc", dacc[:, :], (128, TQ), F32, "h_div"),
        ("kcb", kcb[0][:, :], (128, 512), BF16, "h_div"),
        ("prb", prb[1][:, :], (128, TQ), BF16, "h_div"),
    ]:
        if nm in _DBG:
            t = dout(nm, shp, dt_)
            ctx.wait("sync", sem, ctx.cur(sem))
            ctx.dma("sync", t.ap(), tile, "dbg")

    pps.close()
    hes.close()


def _emit_ln(ctx, nc, sb, prefix, src_ap, sqt, htile, ps_st, ps_bc, strow):
    Tn = htile.shape[2]
    ctx.wait("scalar", prefix + "_ln", ctx.cur(prefix + "_ln"))
    ctx.op("scalar", lambda e: e.activation(sqt[:, :, :], src_ap, AF.Copy),
           sem=prefix + "_ln")
    ctx.wait("tensor", prefix + "_ln", ctx.cur(prefix + "_ln"))
    def sum_mm(e):
        for kc in range(FC):
            r = e.matmul(ps_st[:, 0, :], sb["onesb"][:, :], sqt[:, kc, :],
                         start=(kc == 0), stop=(kc == FC - 1))
        return r
    ctx.op("tensor", sum_mm, sem=prefix + "_ln")
    ctx.wait("vector", prefix + "_ln", ctx.cur(prefix + "_ln"))
    ctx.op("vector", lambda e: e.tensor_mul(sqt[:, :, :], sqt[:, :, :], sqt[:, :, :]),
           sem=prefix + "_ln")
    ctx.wait("tensor", prefix + "_ln", ctx.cur(prefix + "_ln"))
    def sq_mm(e):
        for kc in range(FC):
            r = e.matmul(ps_st[:, 1, :], sb["onesb"][:, :], sqt[:, kc, :],
                         start=(kc == 0), stop=(kc == FC - 1))
        return r
    ctx.op("tensor", sq_mm, sem=prefix + "_ln")
    ctx.wait("vector", prefix + "_ln", ctx.cur(prefix + "_ln"))
    def statcalc(e):
        e.tensor_scalar_mul(strow[:, 0, :], ps_st[:, 0, :], 1.0 / D)
        e.tensor_scalar_mul(strow[:, 2, :], ps_st[:, 1, :], 1.0 / D)
        e.tensor_mul(strow[:, 1, :], strow[:, 0, :], strow[:, 0, :])
        e.tensor_sub(strow[:, 2, :], strow[:, 2, :], strow[:, 1, :])
        return e.tensor_scalar_add(strow[:, 2, :], strow[:, 2, :], 1e-5)
    ctx.op("vector", statcalc, sem=prefix + "_ln")
    ctx.wait("scalar", prefix + "_ln", ctx.cur(prefix + "_ln"))
    ctx.op("scalar", lambda e: e.activation(strow[:, 2, :], strow[:, 2, :], AF.Sqrt),
           sem=prefix + "_ln")
    ctx.wait("vector", prefix + "_ln", ctx.cur(prefix + "_ln"))
    ctx.op("vector", lambda e: e.reciprocal(strow[:, 1, :], strow[:, 2, :]),
           sem=prefix + "_ln")
    ctx.wait("tensor", prefix + "_ln", ctx.cur(prefix + "_ln"))
    def bcast(e):
        e.matmul(ps_bc[:, 0, :], sb["ones1f"][:, :], strow[:, 0, :], start=True, stop=True)
        return e.matmul(ps_bc[:, 1, :], sb["ones1f"][:, :], strow[:, 1, :],
                        start=True, stop=True)
    ctx.op("tensor", bcast, sem=prefix + "_ln")
    ctx.wait("vector", prefix + "_ln", ctx.cur(prefix + "_ln"))
    def happly(e):
        mb = ps_bc[:, 0:1, :].broadcast_to([128, FC, Tn])
        rb = ps_bc[:, 1:2, :].broadcast_to([128, FC, Tn])
        e.tensor_sub(htile[:, :, :], src_ap, mb)
        return e.tensor_mul(htile[:, :, :], htile[:, :, :], rb)
    ctx.op("vector", happly, sem=prefix + "_ln")


def _p_ffn(ctx, nc, es, dr, sb, dout, salloc, tag):
    resid = sb["resid"]
    OGW = 256
    NOG = FF // OGW
    NHC = FF // 128

    fes = ExitStack()
    htile = fes.enter_context(nc.sbuf_tensor(tag + "h", [128, FC, 512], BF16))
    strow = fes.enter_context(nc.sbuf_tensor(tag + "st", [1, 3, 512], F32))
    x2 = fes.enter_context(nc.sbuf_tensor(tag + "x2", [128, NHC, 512], BF16))
    hA = fes.enter_context(nc.sbuf_tensor(tag + "hA", [16, 512], BF16))
    x2A = fes.enter_context(nc.sbuf_tensor(tag + "x2A", [16, 512], BF16))
    aupt = fes.enter_context(nc.sbuf_tensor(tag + "aup", [128, FC, LR], BF16))
    adnt = fes.enter_context(nc.sbuf_tensor(tag + "adn", [128, NHC, LR], BF16))
    bupt = fes.enter_context(nc.sbuf_tensor(tag + "bup", [128, NHC], F32))
    bdnt = fes.enter_context(nc.sbuf_tensor(tag + "bdn", [128, FC], F32))

    pps = ExitStack()
    ps_bc = pps.enter_context(nc.psum_tensor(tag + "pbc", [128, 2, 512], F32))
    ps_st = ps_bc[0:1, :, :]
    ps_y1 = [pps.enter_context(nc.psum_tensor(tag + f"py1{i}", [128, 512], F32))
             for i in range(2)]
    ps_hA = pps.enter_context(nc.psum_tensor(tag + "phA", [16, 512], F32))
    ps_y2 = [pps.enter_context(nc.psum_tensor(tag + f"py2{i}", [128, 512], F32))
             for i in range(2)]

    pfx = tag
    nro = ctx.cur("resid_ok")
    ctx.wait("gpsimd", "resid_ok", nro)
    ctx.wait("sync", "resid_ok", nro)
    ctx.dma("gpsimd", aupt[:, :, :], dr["aup"].ap().rearrange("(c p) r -> p c r", p=128),
            pfx + "_cld")
    ctx.dma("gpsimd", adnt[:, :, :], dr["adn"].ap().rearrange("(c p) r -> p c r", p=128),
            pfx + "_cld")
    ctx.dma("sync", bupt[:, :], dr["bup"].ap(), pfx + "_cld")
    ctx.dma("sync", bdnt[:, :], dr["bdn"].ap(), pfx + "_cld")
    n_cld = ctx.cur(pfx + "_cld")

    for ps in range(2):
        tsl = slice(ps * 512, (ps + 1) * 512)
        ctx.wait("scalar", "resid_ok", nro)
        if ps == 1:
            ctx.wait("scalar", pfx + "_y2add", ctx.cur(pfx + "_y2add"))
            ctx.wait("scalar", pfx + "_ln", ctx.cur(pfx + "_ln"))
        lns = ExitStack()
        sqt = lns.enter_context(nc.sbuf_tensor(tag + f"sq{ps}", [128, FC, 512], BF16))
        _emit_ln(ctx, nc, sb, pfx, resid[:, :, tsl], sqt, htile, ps_st, ps_bc, strow)
        lns.close()

        ctx.wait("tensor", pfx + "_ln", ctx.cur(pfx + "_ln"))
        if ps == 0:
            ctx.wait("tensor", pfx + "_cld", n_cld)
        ctx.wait("tensor", pfx + "_hAe", ctx.cur(pfx + "_hAe"))
        ctx.wait("tensor", pfx + "_x2Ae", ctx.cur(pfx + "_x2Ae"))
        def hA_mm(e):
            for kc in range(FC):
                r = e.matmul(ps_hA[:, :], aupt[:, kc, :], htile[:, kc, :],
                             start=(kc == 0), stop=(kc == FC - 1))
            return r
        ctx.op("tensor", hA_mm, sem=pfx + "_hAmm")
        ctx.wait("scalar", pfx + "_hAmm", ctx.cur(pfx + "_hAmm"))
        ctx.op("scalar", lambda e: e.activation(hA[:, :], ps_hA[:, :], AF.Copy),
               sem=pfx + "_hAe")

        ctx.wait("gpsimd", pfx + "_qmm", ctx.cur(pfx + "_qmm"))
        ctx.wait("gpsimd", pfx + "_ln", ctx.cur(pfx + "_ln"))
        ups = ExitStack()
        wq8 = [ups.enter_context(nc.sbuf_tensor(tag + f"wq8_{ps}_{i}", [128, FC, OGW], I8))
               for i in range(2)]
        sct = ups.enter_context(nc.sbuf_tensor(tag + f"sct_{ps}", [128, FC, OGW], F32))
        wbb = [ups.enter_context(nc.sbuf_tensor(tag + f"wbb_{ps}_{i}", [128, FC, OGW], BF16))
               for i in range(3)]
        bplb = [ups.enter_context(nc.sbuf_tensor(tag + f"bpl_{ps}_{i}", [16, OGW], BF16))
                for i in range(2)]
        for og in range(NOG):
            gb = og % 2
            osl = slice(og * OGW, (og + 1) * OGW)
            if _PAR_A:
                ctx.wait("gpsimd", pfx + "_y1ev", ctx.cur(pfx + "_y1ev"))
            if og >= 2:
                ctx.wait("gpsimd", pfx + "_dq", ctx.cur(pfx + "_dq") - 1)
            ctx.dma("gpsimd", wq8[gb][:, :, :], dr["wupT"].ap()[og:og + 1].squeeze(0), pfx + "_wld")
            if og >= 1:
                ctx.wait("gpsimd", pfx + "_dq", ctx.cur(pfx + "_dq"))
            for pg in range(2):
                ctx.dma("gpsimd", sct[pg * 64:(pg + 1) * 64, :, :],
                        dr["supT"].ap().rearrange("(k g) o -> g k o", g=2)[pg][:, osl]
                        .unsqueeze(0).broadcast_to([64, FC, OGW]), pfx + "_wld")
            if og >= 2:
                ctx.wait("gpsimd", pfx + "_y1mm", ctx.cur(pfx + "_y1mm") - 1)
            ctx.dma("gpsimd", bplb[gb][:, :], dr["bupl"].ap()[:, osl], pfx + "_wld")
            ctx.wait("vector", pfx + "_wld", ctx.cur(pfx + "_wld"))
            ctx.wait("vector", pfx + "_y1mm", ctx.cur(pfx + "_y1mm"))
            ctx.op("vector", lambda e, g=gb, g3=og % 3:
                   e.scalar_tensor_tensor(wbb[g3][:, :, :], wq8[g][:, :, :], -7.5,
                                          sct[:, :, :], ALU.add, ALU.mult),
                   sem=pfx + "_dq")
            ctx.wait("tensor", pfx + "_dq", ctx.cur(pfx + "_dq"))
            ctx.wait("tensor", pfx + "_hAe", ctx.cur(pfx + "_hAe"))
            if og >= 1:
                ctx.wait("tensor", pfx + "_y1ev", ctx.cur(pfx + "_y1ev"))
            def y1_mm(e, g3=og % 3, g=gb):
                for oc2 in range(2):
                    for kc in range(FC):
                        e.matmul(ps_y1[oc2][:, :],
                                 wbb[g3][:, kc, oc2 * 128:(oc2 + 1) * 128],
                                 htile[:, kc, :], start=(kc == 0), stop=False)
                    r = e.matmul(ps_y1[oc2][:, :],
                                 bplb[g][:, oc2 * 128:(oc2 + 1) * 128],
                                 hA[:, :], start=False, stop=True)
                return r
            ctx.op("tensor", y1_mm, sem=pfx + "_y1mm")
            ctx.wait("scalar", pfx + "_y1mm", ctx.cur(pfx + "_y1mm"))
            if og == 0 and ps == 0:
                ctx.wait("scalar", pfx + "_cld", n_cld)
            def y1_ev(e, o=og):
                for oc2 in range(2):
                    oc = o * 2 + oc2
                    r = e.activation(x2[:, oc, :], ps_y1[oc2][:, :], AF.Relu,
                                     bias=bupt[:, oc:oc + 1])
                return r
            ctx.op("scalar", y1_ev, sem=pfx + "_y1ev", n=2)
        ups.close()

        ctx.wait("tensor", pfx + "_y1ev", ctx.cur(pfx + "_y1ev"))
        ctx.wait("tensor", pfx + "_hAe", ctx.cur(pfx + "_hAe"))
        if ps == 1:
            ctx.wait("tensor", pfx + "_x2Ae", 1)
        def x2A_mm(e):
            for hc in range(NHC):
                r = e.matmul(ps_hA[:, :], adnt[:, hc, :], x2[:, hc, :],
                             start=(hc == 0), stop=(hc == NHC - 1))
            return r
        ctx.op("tensor", x2A_mm, sem=pfx + "_x2Amm")
        ctx.wait("scalar", pfx + "_x2Amm", ctx.cur(pfx + "_x2Amm"))
        ctx.op("scalar", lambda e: e.activation(x2A[:, :], ps_hA[:, :], AF.Copy),
               sem=pfx + "_x2Ae")

        HH = NHC // 2
        ctx.wait("gpsimd", pfx + "_y1mm", ctx.cur(pfx + "_y1mm"))
        dns = ExitStack()
        dq8 = [dns.enter_context(nc.sbuf_tensor(tag + f"dq8_{ps}_{i}", [128, HH, 128], I8))
               for i in range(2)]
        sc2 = dns.enter_context(nc.sbuf_tensor(tag + f"sc2_{ps}", [128, HH, 128], F32))
        wbd = [dns.enter_context(nc.sbuf_tensor(tag + f"wbd_{ps}_{i}", [128, HH, 128], BF16))
               for i in range(3)]
        bdlb = [dns.enter_context(nc.sbuf_tensor(tag + f"bdl_{ps}_{i}", [16, 128], BF16))
                for i in range(2)]
        for dc in range(FC):
            dsl = slice(dc * 128, (dc + 1) * 128)
            pb = dc % 2
            for hh in range(2):
                u = dc * 2 + hh
                gb = u % 2
                if _PAR_B:
                    ctx.wait("gpsimd", pfx + "_y2add", ctx.cur(pfx + "_y2add"))
                    ctx.wait("gpsimd", pfx + "_y2mm", ctx.cur(pfx + "_y2mm"))
                if u >= 2:
                    ctx.wait("gpsimd", pfx + "_ddq", ctx.cur(pfx + "_ddq") - 1)
                ctx.dma("gpsimd", dq8[gb][:, :, :],
                        dr["wdnT"].ap()[dc:dc + 1].squeeze(0)[:, hh * HH:(hh + 1) * HH, :],
                        pfx + "_dld")
                ctx.wait("gpsimd", pfx + "_ddq", ctx.cur(pfx + "_ddq"))
                for pg in range(2):
                    ctx.dma("gpsimd", sc2[pg * 64:(pg + 1) * 64, :, :],
                            dr["sdnT"].ap().rearrange("(k g) o -> g k o", g=2)[pg]
                            [hh * HH:(hh + 1) * HH, dsl]
                            .unsqueeze(0).broadcast_to([64, HH, 128]), pfx + "_dld")
                if hh == 0:
                    ctx.wait("gpsimd", pfx + "_y2mm", max(0, ctx.cur(pfx + "_y2mm") - 2))
                    ctx.dma("gpsimd", bdlb[pb][:, :], dr["bdnl"].ap()[:, dsl],
                            pfx + "_dld")
                ctx.wait("vector", pfx + "_dld", ctx.cur(pfx + "_dld"))
                ctx.wait("vector", pfx + "_y2mm", ctx.cur(pfx + "_y2mm"))
                ctx.op("vector", lambda e, g=gb, g3=u % 3:
                       e.scalar_tensor_tensor(wbd[g3][:, :, :], dq8[g][:, :, :], -7.5,
                                              sc2[:, :, :], ALU.add, ALU.mult),
                       sem=pfx + "_ddq")
                ctx.wait("tensor", pfx + "_ddq", ctx.cur(pfx + "_ddq"))
                if u == 0:
                    ctx.wait("tensor", pfx + "_x2Ae", ctx.cur(pfx + "_x2Ae"))
                if hh == 0 and dc >= 2:
                    ctx.wait("tensor", pfx + "_y2add", ctx.cur(pfx + "_y2add") - 1)
                def y2_mm(e, g3=u % 3, d=dc, h2=hh, p=pb):
                    for hc in range(HH):
                        r = e.matmul(ps_y2[p][:, :], wbd[g3][:, hc, :],
                                     x2[:, h2 * HH + hc, :],
                                     start=(h2 == 0 and hc == 0), stop=False)
                    if h2 == 1:
                        r = e.matmul(ps_y2[p][:, :], bdlb[p][:, :], x2A[:, :],
                                     start=False, stop=True)
                    return r
                ctx.op("tensor", y2_mm, sem=pfx + "_y2mm")
            ctx.wait("vector", pfx + "_y2mm", ctx.cur(pfx + "_y2mm"))
            if dc == 0 and ps == 0:
                ctx.wait("vector", pfx + "_cld", n_cld)
            ctx.op("vector", lambda e, d=dc, t=tsl, p=pb:
                   e.scalar_tensor_tensor(resid[:, d, t], ps_y2[p][:, :],
                                          bdnt[:, d:d + 1],
                                          resid[:, d, t], ALU.add, ALU.add),
                   sem=pfx + "_y2add")
        dns.close()

    if tag + "x2" in _DBG:
        t = dout(tag + "x2", (128, FF // 128, 512), BF16)
        ctx.wait("sync", pfx + "_y1ev", ctx.cur(pfx + "_y1ev"))
        ctx.dma("sync", t.ap(), x2[:, :, :], "dbg")
    if tag + "hd" in _DBG:
        t = dout(tag + "hd", (128, FC, 512), BF16)
        ctx.wait("sync", pfx + "_ln", ctx.cur(pfx + "_ln"))
        ctx.dma("sync", t.ap(), htile[:, :, :], "dbg")
    ctx.wait("sync", pfx + "_y2add", ctx.cur(pfx + "_y2add"))
    ctx.op("sync", lambda e: e.nop(), sem="resid_ok")
    if tag + "res" in _DBG:
        t = dout(tag + "res", (128, FC, TQ), F32)
        ctx.wait("sync", pfx + "_y2add", ctx.cur(pfx + "_y2add"))
        ctx.dma("sync", t.ap(), resid[:, :, :], "dbg")
    pps.close()
    fes.close()


def _p_attn_b(ctx, nc, es, dr, sb, dout, salloc):
    resid = sb["resid"]
    pfx = "b"

    fes = ExitStack()
    htile = fes.enter_context(nc.sbuf_tensor("bh", [128, FC, TQ], BF16))
    ofb = fes.enter_context(nc.sbuf_tensor("bof", [128, H, TQ], BF16))
    kvst = fes.enter_context(nc.sbuf_tensor("bkv", [128, 8, 128], F32))
    strow = fes.enter_context(nc.sbuf_tensor("bst", [1, 3, 512], F32))
    qplr2 = fes.enter_context(nc.sbuf_tensor("bqp2", [1, 2, 512], F32))
    negs = fes.enter_context(nc.sbuf_tensor("bns", [1, 16], F32))
    negsb = fes.enter_context(nc.sbuf_tensor("bnsb", [128, 16], F32))
    bdp = fes.enter_context(nc.sbuf_tensor("bbdp", [128, 4, H], F32))
    kdeg = fes.enter_context(nc.sbuf_tensor("bkdg", [128, 2], F32))
    kde = fes.enter_context(nc.sbuf_tensor("bkde", [128, 2], F32))
    dge = fes.enter_context(nc.sbuf_tensor("bdge", [128, 2, 256], BF16))
    dgm = fes.enter_context(nc.sbuf_tensor("bdgm", [128, 2, 256], BF16))
    decT = fes.enter_context(nc.sbuf_tensor("bdec", [128, 2, 256], F32))
    pmt = fes.enter_context(nc.sbuf_tensor("bpm", [128, 4], F32))
    pc2 = fes.enter_context(nc.sbuf_tensor("bpc2", [1, 2, 64], F32))
    ccoef = fes.enter_context(nc.sbuf_tensor("bcc", [128, 64], F32))

    nro = ctx.cur("resid_ok")
    ctx.wait("gpsimd", "resid_ok", nro)
    ctx.wait("sync", "resid_ok", nro)
    ctx.dma("sync", kdeg[:, :], dr["kdecg"].ap(), pfx + "_cld")
    ctx.dma("gpsimd", dge[:, :, :], dr["dgexp"].ap().rearrange("g p o -> p g o"), pfx + "_cldg")
    ctx.dma("gpsimd", dgm[:, :, :], dr["dgmask"].ap().rearrange("g p o -> p g o"), pfx + "_cldg")
    ctx.dma("sync", qplr2[:, :, :], dr["qposl"].ap().rearrange("a (b t) -> a b t", b=2), pfx + "_cld")
    ctx.dma("sync", pmt[:, :], dr["pmask"].ap(), pfx + "_cld")
    ctx.dma("sync", pc2[:, 0, 0:4], dr["pcoefg"].ap(), pfx + "_cld")
    ctx.dma("sync", pc2[:, 1, 0:4], dr["pcmask"].ap(), pfx + "_cld")
    n_cld = ctx.cur(pfx + "_cld")

    pps0 = ExitStack()
    ps_bc = pps0.enter_context(nc.psum_tensor("bpbc", [128, 2, 512], F32))
    ps_st = ps_bc[0:1, :, :]

    ctx.dma("sync", negs[:, :], dr["slope"].ap(), pfx + "_cld")
    n_cld = ctx.cur(pfx + "_cld")
    ctx.wait("vector", pfx + "_cld", n_cld)
    ctx.op("vector", lambda e: e.tensor_scalar_mul(negs[:, :], negs[:, :], -1.0),
           sem=pfx + "_neg")
    ctx.wait("tensor", pfx + "_neg", ctx.cur(pfx + "_neg"))
    ctx.op("tensor", lambda e: e.matmul(ps_bc[:, 0, 0:16], sb["ones1f"][:, :],
                                        negs[:, :], start=True, stop=True),
           sem=pfx + "_nbc")
    ctx.wait("vector", pfx + "_nbc", ctx.cur(pfx + "_nbc"))
    ctx.op("vector", lambda e: e.tensor_copy(negsb[:, :], ps_bc[:, 0, 0:16]),
           sem=pfx + "_nsb")
    ctx.wait("vector", pfx + "_nsb", ctx.cur(pfx + "_nsb"))
    def bdrow(e):
        for l in range(4):
            r = e.tensor_scalar_mul(strow[:, 0, l * 16:(l + 1) * 16],
                                    negs[:, :], 256.0 * l)
        return r
    ctx.op("vector", bdrow, sem=pfx + "_bdr")
    ctx.wait("scalar", pfx + "_bdr", ctx.cur(pfx + "_bdr"))
    ctx.op("scalar", lambda e: e.activation(strow[:, 0, 0:64], strow[:, 0, 0:64], AF.Exp),
           sem=pfx + "_bde")
    ctx.wait("tensor", pfx + "_bde", ctx.cur(pfx + "_bde"))
    ctx.op("tensor", lambda e: e.matmul(ps_bc[:, 0, 0:64], sb["ones1f"][:, :],
                                        strow[:, 0, 0:64], start=True, stop=True),
           sem=pfx + "_bdb")
    ctx.wait("vector", pfx + "_bdb", ctx.cur(pfx + "_bdb"))
    ctx.op("vector", lambda e: e.tensor_copy(
        bdp[:, :, :].rearrange("p l h -> p (l h)"), ps_bc[:, 0, 0:64]), sem=pfx + "_bdp")
    def pcrow(e):
        for p in range(4):
            e.tensor_scalar_mul(strow[:, 1, p * 16:(p + 1) * 16], negs[:, :], 1.0)
            r = e.tensor_single_scalar(strow[:, 1, p * 16:(p + 1) * 16],
                                       strow[:, 1, p * 16:(p + 1) * 16],
                                       pc2[:, 0, p:p + 1], ALU.mult)
        return r
    ctx.op("vector", pcrow, sem=pfx + "_pcr")
    ctx.wait("scalar", pfx + "_pcr", ctx.cur(pfx + "_pcr"))
    ctx.op("scalar", lambda e: e.activation(strow[:, 1, 0:64], strow[:, 1, 0:64], AF.Exp),
           sem=pfx + "_pce")
    ctx.wait("vector", pfx + "_pce", ctx.cur(pfx + "_pce"))
    def pcmask_mul(e):
        for p in range(4):
            r = e.tensor_single_scalar(strow[:, 1, p * 16:(p + 1) * 16],
                                       strow[:, 1, p * 16:(p + 1) * 16],
                                       pc2[:, 1, p:p + 1], ALU.mult)
        return r
    ctx.op("vector", pcmask_mul, sem=pfx + "_pcm")
    ctx.wait("tensor", pfx + "_pcm", ctx.cur(pfx + "_pcm"))
    ctx.op("tensor", lambda e: e.matmul(ps_bc[:, 1, 0:64], sb["ones1f"][:, :],
                                        strow[:, 1, 0:64], start=True, stop=True),
           sem=pfx + "_ccb")
    ctx.wait("vector", pfx + "_ccb", ctx.cur(pfx + "_ccb"))
    ctx.op("vector", lambda e: e.tensor_copy(ccoef[:, :], ps_bc[:, 1, 0:64]),
           sem=pfx + "_cc")

    lns = ExitStack()
    sqt = lns.enter_context(nc.sbuf_tensor("bsq", [128, FC, 512], BF16))
    ctx.wait("tensor", pfx + "_cc", ctx.cur(pfx + "_cc"))
    ctx.wait("vector", pfx + "_cc", ctx.cur(pfx + "_cc"))
    for hf in range(2):
        tsl = slice(hf * 512, (hf + 1) * 512)
        ctx.wait("scalar", "resid_ok", nro)
        _emit_ln(ctx, nc, sb, pfx + f"ln{hf}", resid[:, :, tsl], sqt,
                 htile[:, :, tsl], ps_st, ps_bc, strow)
    ctx.wait("scalar", pfx + "ln1_ln", ctx.cur(pfx + "ln1_ln"))
    ctx.op("scalar", lambda e: e.activation(htile[:, :, :], htile[:, :, :], AF.Copy),
           sem=pfx + "_hs")
    lns.close()
    pps0.close()

    if _BCUT <= 1:
        fes.close()
        return

    HG = 8
    kvgd = [nc.dram_tensor(f"kvgd{g}", [4, 128, HG * 128], F32, kind="Internal")
            for g in range(2)]
    qes = ExitStack()
    qdq = qes.enter_context(nc.sbuf_tensor("bqdq", [128, HG, TQ], BF16))
    wqs = [qes.enter_context(nc.sbuf_tensor(f"bwq{i}", [128, FC, 3 * HD], BF16))
           for i in range(2)]
    qf = qes.enter_context(nc.sbuf_tensor("bqf", [128, TQ], BF16))
    kf = qes.enter_context(nc.sbuf_tensor("bkf", [128, TQ], BF16))
    vt = qes.enter_context(nc.sbuf_tensor("bvt", [128, 8, 128], BF16))
    ktm = qes.enter_context(nc.sbuf_tensor("bktm", [128, 8, 128], BF16))
    ktp = qes.enter_context(nc.sbuf_tensor("bktp", [128, 2, 128], BF16))
    prbB = qes.enter_context(nc.sbuf_tensor("bprb", [128, 2, 256], BF16))
    kvbf = qes.enter_context(nc.sbuf_tensor("bkvb", [128, 128], BF16))
    kvin = qes.enter_context(nc.sbuf_tensor("bkvi", [128, HG, 128], F32))
    kstg = qes.enter_context(nc.sbuf_tensor("bkst", [128, HG * 128], F32))

    pps = ExitStack()
    ps_a = [pps.enter_context(nc.psum_tensor(f"bpa{i}", [128, 512], F32)) for i in range(2)]
    ps_sc = pps.enter_context(nc.psum_tensor("bpsc", [128, 2, 256], F32))
    ps_o = pps.enter_context(nc.psum_tensor("bpo", [128, 256], F32))
    ps_kv = pps.enter_context(nc.psum_tensor("bpkv", [128, 128], F32))
    ps_vt = pps.enter_context(nc.psum_tensor("bpvt", [128, 128], F32))
    ps_tr = pps.enter_context(nc.psum_tensor("bptr", [128, 128], BF16))

    def head_w(h, bi):
        base = h * 3 * HD
        ctx.dma("gpsimd", wqs[bi][:, :, :],
                dr["wqkvT"].ap().rearrange("(c p) o -> p c o", p=128)
                [:, :, base:base + 3 * HD], pfx + "_w")

    head_w(0, 0)

    for hg in range(2):
        for hl in range(HG):
            h = hg * HG + hl
            bi = h % 2
            if h + 1 < H:
                if h >= 1:
                    ctx.wait("gpsimd", pfx + "_qkmm", 2 * h)
                    ctx.wait("gpsimd", pfx + "_vtmm", 8 * h)
                head_w(h + 1, 1 - bi)

            ctx.wait("tensor", pfx + "_w", 16 * (h + 1))
            if h == 0:
                ctx.wait("tensor", pfx + "_hs", 1)
                ctx.wait("tensor", pfx + "_cc", ctx.cur(pfx + "_cc"))
            if h >= 1:
                ctx.wait("tensor", pfx + "_qfe", 2 * h)
                ctx.wait("tensor", pfx + "_qdq", h)
            def qk_mm(e, b=bi, which=0):
                r = None
                for half in range(2):
                    n = slice(half * 512, (half + 1) * 512)
                    for kc in range(FC):
                        r = e.matmul(ps_a[half][:, :], wqs[b][:, kc, which * HD:(which + 1) * HD],
                                     htile[:, kc, n], start=(kc == 0), stop=(kc == FC - 1))
                return r
            ctx.op("tensor", qk_mm, sem=pfx + "_qkmm")
            ctx.wait("scalar", pfx + "_qkmm", ctx.cur(pfx + "_qkmm"))
            def qf_ev(e):
                e.activation(qf[:, 0:512], ps_a[0][:, :], AF.Silu)
                return e.activation(qf[:, 512:1024], ps_a[1][:, :], AF.Silu)
            ctx.op("scalar", qf_ev, sem=pfx + "_qfe", n=1)
            ctx.wait("tensor", pfx + "_qfe", ctx.cur(pfx + "_qfe"))
            def kf_mm(e, b=bi):
                return qk_mm(e, b, 1)
            ctx.op("tensor", kf_mm, sem=pfx + "_qkmm")
            ctx.wait("scalar", pfx + "_qkmm", ctx.cur(pfx + "_qkmm"))
            def kf_ev(e):
                e.activation(kf[:, 0:512], ps_a[0][:, :], AF.Silu)
                return e.activation(kf[:, 512:1024], ps_a[1][:, :], AF.Silu)
            ctx.op("scalar", kf_ev, sem=pfx + "_qfe", n=1)

            for tc in range(8):
                ctx.wait("tensor", pfx + "_vte", ctx.cur(pfx + "_vte"))
                def vt_mm(e, b=bi, t=tc):
                    r = None
                    for kc in range(FC):
                        r = e.matmul(ps_vt[:, :], htile[:, kc, t * 128:(t + 1) * 128],
                                     wqs[b][:, kc, 2 * HD:3 * HD],
                                     start=(kc == 0), stop=(kc == FC - 1))
                    return r
                ctx.op("tensor", vt_mm, sem=pfx + "_vtmm")
                ctx.wait("scalar", pfx + "_vtmm", ctx.cur(pfx + "_vtmm"))
                ctx.op("scalar", lambda e, t=tc:
                       e.activation(vt[:, t, :], ps_vt[:, :], AF.Silu), sem=pfx + "_vte")

            ctx.wait("tensor", pfx + "_qfe", ctx.cur(pfx + "_qfe"))
            for tc in range(8):
                ctx.wait("tensor", pfx + "_kte", ctx.cur(pfx + "_kte"))
                ctx.op("tensor", lambda e, t=tc:
                       e.transpose(ps_tr[:, :], kf[:, t * 128:(t + 1) * 128],
                                   sb["identt"][:, :]), sem=pfx + "_ktr")
                ctx.wait("scalar", pfx + "_ktr", ctx.cur(pfx + "_ktr"))
                ctx.op("scalar", lambda e, t=tc:
                       e.activation(ktm[:, t, :], ps_tr[:, :], AF.Copy), sem=pfx + "_kte")

            ctx.wait("scalar", pfx + "_prb", ctx.cur(pfx + "_prb"))
            if h == 0:
                ctx.wait("scalar", pfx + "_nsb", ctx.cur(pfx + "_nsb"))
                ctx.wait("scalar", pfx + "_cldg", ctx.cur(pfx + "_cldg"))
            ctx.op("scalar", lambda e, hh=h:
                   e.activation(decT[:, :, :], dge[:, :, :], AF.Exp,
                                scale=negsb[:, hh:hh + 1]), sem=pfx + "_dce")
            ctx.wait("scalar", pfx + "_kvmm", ctx.cur(pfx + "_kvmm"))
            ctx.op("scalar", lambda e, hh=h:
                   e.activation(kde[:, :], kdeg[:, :], AF.Exp, scale=negsb[:, hh:hh + 1]),
                   sem=pfx + "_kdee")
            ctx.wait("vector", pfx + "_dce", ctx.cur(pfx + "_dce"))
            ctx.wait("vector", pfx + "_prb", ctx.cur(pfx + "_prb"))
            ctx.op("vector", lambda e: e.tensor_mul(decT[:, :, :], decT[:, :, :],
                                                    dgm[:, :, :]), sem=pfx + "_dcm")
            ctx.wait("scalar", pfx + "_qdb", ctx.cur(pfx + "_qdb"))
            ctx.op("scalar", lambda e, hh=h:
                   e.activation(strow[:, 1, :], qplr2[:, 0, :],
                                AF.Exp, scale=negs[:, hh:hh + 1]), sem=pfx + "_qde")
            ctx.op("scalar", lambda e, hh=h:
                   e.activation(strow[:, 2, :], qplr2[:, 1, :], AF.Exp,
                                scale=negs[:, hh:hh + 1]), sem=pfx + "_qde")
            ctx.wait("tensor", pfx + "_qde", ctx.cur(pfx + "_qde"))
            ctx.wait("tensor", pfx + "_qkmm", ctx.cur(pfx + "_qkmm"))
            ctx.wait("tensor", pfx + "_qfe", ctx.cur(pfx + "_qfe"))
            def qd_bc(e):
                e.matmul(ps_a[0][:, :], sb["ones1f"][:, :], strow[:, 1, :],
                         start=True, stop=True)
                return e.matmul(ps_a[1][:, :], sb["ones1f"][:, :], strow[:, 2, :],
                                start=True, stop=True)
            ctx.op("tensor", qd_bc, sem=pfx + "_qdb")
            ctx.wait("vector", pfx + "_qdb", ctx.cur(pfx + "_qdb"))
            def qdq_mk(e, hh=hl):
                e.tensor_mul(qdq[:, hh, 0:512], qf[:, 0:512], ps_a[0][:, :])
                return e.tensor_mul(qdq[:, hh, 512:1024], qf[:, 512:1024], ps_a[1][:, :])
            ctx.op("vector", qdq_mk, sem=pfx + "_qdq")

            for l in range(4 if _BCUT >= 3 else 0):
                msl = slice(l * 256, (l + 1) * 256)
                ctx.wait("tensor", pfx + "_prb", ctx.cur(pfx + "_prb"))
                ctx.wait("tensor", pfx + "_kte", ctx.cur(pfx + "_kte"))
                def sc_mm(e, ll=l):
                    r = None
                    for nc2 in range(2):
                        nchunk = ll * 256 + nc2 * 128
                        r = e.matmul(ps_sc[:, nc2, :], kf[:, nchunk:nchunk + 128],
                                     qf[:, ll * 256:(ll + 1) * 256], start=True, stop=True)
                    return r
                ctx.op("tensor", sc_mm, sem=pfx + "_scmm")
                ctx.wait("vector", pfx + "_scmm", ctx.cur(pfx + "_scmm"))
                ctx.wait("vector", pfx + "_dcm", ctx.cur(pfx + "_dcm"))
                ctx.wait("vector", pfx + "_omm", ctx.cur(pfx + "_omm"))
                ctx.op("vector", lambda e:
                       e.tensor_mul(prbB[:, :, :], ps_sc[:, :, :], decT[:, :, :]),
                       sem=pfx + "_prb")
                ctx.wait("tensor", pfx + "_prb", ctx.cur(pfx + "_prb"))
                ctx.wait("tensor", pfx + "_vte", ctx.cur(pfx + "_vte"))
                ctx.wait("tensor", pfx + "_oev", ctx.cur(pfx + "_oev"))
                ctx.wait("tensor", pfx + "_cora", ctx.cur(pfx + "_cora"))
                if l >= 1:
                    ctx.wait("tensor", pfx + "_kvbf", ctx.cur(pfx + "_kvbf"))
                if l == 0:
                    def o_mm(e, ll=l):
                        e.matmul(ps_o[:, :], vt[:, 0, :], prbB[:, 0, :],
                                 start=True, stop=False)
                        return e.matmul(ps_o[:, :], vt[:, 1, :], prbB[:, 1, :],
                                        start=False, stop=True)
                else:
                    def o_mm(e, ll=l, hh=hl):
                        for nc2 in range(2):
                            e.matmul(ps_o[:, :], vt[:, ll * 2 + nc2, :], prbB[:, nc2, :],
                                     start=(nc2 == 0), stop=False)
                        return e.matmul(ps_o[:, :], kvbf[:, :],
                                        qdq[:, hh, ll * 256:(ll + 1) * 256],
                                        start=False, stop=True)
                ctx.op("tensor", o_mm, sem=pfx + "_omm")
                ctx.wait("scalar", pfx + "_omm", ctx.cur(pfx + "_omm"))
                ctx.op("scalar", lambda e, hh=h, m=msl:
                       e.activation(ofb[:, hh, m], ps_o[:, :], AF.Copy), sem=pfx + "_oev")
                ctx.wait("scalar", pfx + "_kvmm", ctx.cur(pfx + "_kvmm"))
                def ktp_mk(e, ll=l):
                    for nc2 in range(2):
                        r = e.activation(ktp[:, nc2, :], ktm[:, ll * 2 + nc2, :], AF.Copy,
                                         scale=kde[:, nc2:nc2 + 1])
                    return r
                ctx.op("scalar", ktp_mk, sem=pfx + "_ktp")
                ctx.wait("tensor", pfx + "_ktp", ctx.cur(pfx + "_ktp"))
                ctx.wait("tensor", pfx + "_kvup", ctx.cur(pfx + "_kvup"))
                def kv_mm(e, ll=l):
                    e.matmul(ps_kv[:, :], ktp[:, 0, :], vt[:, ll * 2, :],
                             start=True, stop=False)
                    return e.matmul(ps_kv[:, :], ktp[:, 1, :], vt[:, ll * 2 + 1, :],
                                    start=False, stop=True)
                ctx.op("tensor", kv_mm, sem=pfx + "_kvmm")
                ctx.wait("vector", pfx + "_kvmm", ctx.cur(pfx + "_kvmm"))
                first_blk = (l == 0)
                def kv_up(e, hh=hl, f=first_blk, gh=h):
                    if f:
                        return e.tensor_copy(kvst[:, hh, :], ps_kv[:, :])
                    return e.scalar_tensor_tensor(kvst[:, hh, :], kvst[:, hh, :],
                                                  bdp[:, 1, gh:gh + 1], ps_kv[:, :],
                                                  ALU.mult, ALU.add)
                ctx.op("vector", kv_up, sem=pfx + "_kvup")
                if l < 3:
                    ctx.wait("scalar", pfx + "_kvup", ctx.cur(pfx + "_kvup"))
                    ctx.wait("scalar", pfx + "_cor_mm", ctx.cur(pfx + "_cor_mm"))
                    ctx.op("scalar", lambda e, hh=hl:
                           e.activation(kvbf[:, :], kvst[:, hh, :], AF.Copy),
                           sem=pfx + "_kvbf")

        n_kvup = ctx.cur(pfx + "_kvup")
        for p in range(4 if _BCUT >= 4 else 0):
            ctx.wait("vector", pfx + "_kvup", n_kvup)
            ctx.wait("vector", pfx + "_stgd", ctx.cur(pfx + "_stgd"))
            ctx.op("vector", lambda e, pp=p:
                   e.tensor_scalar_mul(kstg[:, :],
                                       kvst[:, :, :].rearrange("p h v -> p (h v)"),
                                       pmt[:, pp:pp + 1]), sem=pfx + "_stg")
            ctx.wait("sync", pfx + "_stg", ctx.cur(pfx + "_stg"))
            ctx.dma("sync", kvgd[hg].ap()[p:p + 1].squeeze(0), kstg[:, :], pfx + "_stgd")
        if _BCUT < 4:
            continue
        ctx.wait("gpsimd", pfx + "_stgd", ctx.cur(pfx + "_stgd"))
        def coll(e, g=hg):
            return e.collective_compute(
                "AllReduce", ALU.add,
                replica_groups=[[0, 1, 2, 3], [4, 5, 6, 7]],
                ins=[kvgd[g].ap()[:, :, :]],
                outs=[kvgd[g].ap()[:, :, :]],
                unique_tensors="No")
        ctx.op("gpsimd", coll, sem=pfx + "_cc2")
        ctx.wait("gpsimd", pfx + "_cc2", ctx.cur(pfx + "_cc2"))
        for p in range(4):
            ctx.wait("gpsimd", pfx + "_kvc", ctx.cur(pfx + "_kvc"))
            ctx.dma("gpsimd", kstg[:, :], kvgd[hg].ap()[p:p + 1].squeeze(0), pfx + "_kvr")
            ctx.wait("vector", pfx + "_kvr", ctx.cur(pfx + "_kvr"))
            ctx.wait("vector", pfx + "_corb", ctx.cur(pfx + "_corb"))
            def kv_comb(e, pp=p, g=hg):
                r = None
                for hh in range(HG):
                    c = ccoef[:, pp * 16 + g * HG + hh:pp * 16 + g * HG + hh + 1]
                    if pp == 0:
                        r = e.tensor_scalar_mul(kvin[:, hh, :],
                                                kstg[:, hh * 128:(hh + 1) * 128], c)
                    else:
                        r = e.scalar_tensor_tensor(kvin[:, hh, :],
                                                   kstg[:, hh * 128:(hh + 1) * 128], c,
                                                   kvin[:, hh, :], ALU.mult, ALU.add)
                return r
            ctx.op("vector", kv_comb, sem=pfx + "_kvc")

        n_kvc = ctx.cur(pfx + "_kvc")
        for hl in range(HG if _BCUT >= 5 else 0):
            h = hg * HG + hl
            for l in range(4):
                ctx.wait("scalar", pfx + "_kvc", n_kvc)
                ctx.wait("scalar", pfx + "_cor_mm", ctx.cur(pfx + "_cor_mm"))
                ctx.op("scalar", lambda e, hh=hl, gh=h, ll=l:
                       e.activation(kvbf[:, :], kvin[:, hh, :], AF.Copy,
                                    scale=bdp[:, ll, gh:gh + 1]), sem=pfx + "_corb")
                ctx.wait("tensor", pfx + "_corb", ctx.cur(pfx + "_corb"))
                ctx.wait("tensor", pfx + "_cora", ctx.cur(pfx + "_cora"))
                ctx.op("tensor", lambda e, hh=hl, ll=l:
                       e.matmul(ps_o[:, :], kvbf[:, :],
                                qdq[:, hh, ll * 256:(ll + 1) * 256],
                                start=True, stop=True), sem=pfx + "_cor_mm")
                ctx.wait("vector", pfx + "_cor_mm", ctx.cur(pfx + "_cor_mm"))
                ctx.op("vector", lambda e, gh=h, ll=l:
                       e.tensor_add(ofb[:, gh, ll * 256:(ll + 1) * 256],
                                    ofb[:, gh, ll * 256:(ll + 1) * 256], ps_o[:, :]),
                       sem=pfx + "_cora")
        if hg == 0:
            ctx.wait("vector", pfx + "_stgd", ctx.cur(pfx + "_stgd"))

    pps.close()
    qes.close()

    if "bof" in _DBG:
        t = dout("bof", (128, H, TQ), BF16)
        ctx.wait("sync", pfx + "_cora", ctx.cur(pfx + "_cora"))
        ctx.dma("sync", t.ap(), ofb[:, :, :], "dbg")
        ctx.wait("vector", "dbg", ctx.cur("dbg"))

    if _BCUT <= 5:
        fes.close()
        return

    oes = ExitStack()
    normw = oes.enter_context(nc.sbuf_tensor("bnw", [128, FC], F32))
    osq = oes.enter_context(nc.sbuf_tensor("bosq", [128, H, 512], BF16))
    gtmp = oes.enter_context(nc.sbuf_tensor("bgt", [128, 512], BF16))
    wgs = [oes.enter_context(nc.sbuf_tensor(f"bwg{i}", [128, FC, 128], BF16)) for i in range(2)]
    wos = [oes.enter_context(nc.sbuf_tensor(f"bwo{i}", [128, FC, 128], BF16)) for i in range(2)]
    rrow = oes.enter_context(nc.sbuf_tensor("brr", [1, 2, 512], F32))
    pps2 = ExitStack()
    ps_g = [pps2.enter_context(nc.psum_tensor(f"bpg{i}", [128, 512], F32)) for i in range(2)]
    ps_r = pps2.enter_context(nc.psum_tensor("bpr", [128, 512], F32))
    ps_y = [pps2.enter_context(nc.psum_tensor(f"bpy{i}", [128, 512], F32)) for i in range(2)]

    ctx.wait("gpsimd", pfx + "_cora", ctx.cur(pfx + "_cora"))
    ctx.wait("sync", pfx + "_cora", ctx.cur(pfx + "_cora"))
    ctx.dma("sync", normw[:, :], dr["normw"].ap(), pfx + "_nw")
    n_cora = ctx.cur(pfx + "_cora")

    for hf in range(2):
        tsl = slice(hf * 512, (hf + 1) * 512)
        ctx.wait("vector", pfx + "_cora", n_cora)
        ctx.wait("vector", pfx + "_rmm", ctx.cur(pfx + "_rmm"))
        ctx.op("vector", lambda e, t=tsl:
               e.tensor_mul(osq[:, :, :], ofb[:, :, t], ofb[:, :, t]), sem=pfx + "_osq")
        ctx.wait("tensor", pfx + "_osq", ctx.cur(pfx + "_osq"))
        ctx.wait("tensor", pfx + "_rr", ctx.cur(pfx + "_rr"))
        ctx.wait("tensor", pfx + "_gom", ctx.cur(pfx + "_gom"))
        def rms_mm(e):
            for hh in range(H):
                r = e.matmul(ps_r[0:1, :], sb["onesb"][:, :], osq[:, hh, :],
                             start=(hh == 0), stop=(hh == H - 1))
            return r
        ctx.op("tensor", rms_mm, sem=pfx + "_rmm")
        ctx.wait("vector", pfx + "_rmm", ctx.cur(pfx + "_rmm"))
        def rcalc(e, f=hf):
            e.tensor_scalar_mul(rrow[:, 0, :], ps_r[0:1, :], 1.0 / D)
            return e.tensor_scalar_add(rrow[:, 0, :], rrow[:, 0, :], 1e-6)
        ctx.op("vector", rcalc, sem=pfx + "_rc")
        ctx.wait("scalar", pfx + "_rc", ctx.cur(pfx + "_rc"))
        ctx.op("scalar", lambda e: e.activation(rrow[:, 0, :], rrow[:, 0, :], AF.Sqrt),
               sem=pfx + "_rs")
        ctx.wait("vector", pfx + "_rs", ctx.cur(pfx + "_rs"))
        ctx.op("vector", lambda e: e.reciprocal(rrow[:, 1, :], rrow[:, 0, :]),
               sem=pfx + "_rrec")
        ctx.wait("tensor", pfx + "_rrec", ctx.cur(pfx + "_rrec"))
        ctx.op("tensor", lambda e: e.matmul(ps_r[:, :], sb["ones1f"][:, :], rrow[:, 1, :],
                                            start=True, stop=True), sem=pfx + "_rr")

        for h in range(H):
            gb = h % 2
            if True:
                ctx.wait("gpsimd", pfx + "_gmm", max(0, ctx.cur(pfx + "_gmm") - 1))
                ctx.dma("gpsimd", wgs[gb][:, :, :],
                        dr["wgateT"].ap().rearrange("(c p) o -> p c o", p=128)
                        [:, :, h * 128:(h + 1) * 128], pfx + "_gw")
            ctx.wait("tensor", pfx + "_gw", ctx.cur(pfx + "_gw"))
            ctx.wait("tensor", pfx + "_gev", max(0, ctx.cur(pfx + "_gev") - 1))
            def g_mm(e, b=gb, t=tsl):
                r = None
                for kc in range(FC):
                    r = e.matmul(ps_g[b][:, :], wgs[b][:, kc, :], htile[:, kc, t],
                                 start=(kc == 0), stop=(kc == FC - 1))
                return r
            ctx.op("tensor", g_mm, sem=pfx + "_gmm")
            ctx.wait("scalar", pfx + "_gmm", ctx.cur(pfx + "_gmm"))
            ctx.wait("scalar", pfx + "_gof", ctx.cur(pfx + "_gof"))
            ctx.op("scalar", lambda e, b=gb:
                   e.activation(gtmp[:, :], ps_g[b][:, :], AF.Sigmoid), sem=pfx + "_gev")
            ctx.wait("vector", pfx + "_gev", ctx.cur(pfx + "_gev"))
            ctx.wait("vector", pfx + "_rr", ctx.cur(pfx + "_rr"))
            def go_mk(e, hh=h, t=tsl):
                e.tensor_mul(gtmp[:, :], gtmp[:, :], ofb[:, hh, t])
                return e.tensor_mul(gtmp[:, :], gtmp[:, :], ps_r[:, :])
            ctx.op("vector", go_mk, sem=pfx + "_gom")
            ctx.wait("scalar", pfx + "_gom", ctx.cur(pfx + "_gom"))
            ctx.op("scalar", lambda e, hh=h, t=tsl:
                   e.activation(ofb[:, hh, t], gtmp[:, :], AF.Copy,
                                scale=normw[:, hh:hh + 1]), sem=pfx + "_gof")

        for fc in range(FC):
            wb = fc % 2
            ctx.wait("gpsimd", pfx + "_ymm", max(0, ctx.cur(pfx + "_ymm") - 1))
            ctx.dma("gpsimd", wos[wb][:, :, :],
                    dr["woutT"].ap().rearrange("(c p) o -> p c o", p=128)
                    [:, :, fc * 128:(fc + 1) * 128], pfx + "_yw")
            ctx.wait("tensor", pfx + "_yw", ctx.cur(pfx + "_yw"))
            if fc == 0:
                ctx.wait("tensor", pfx + "_gof", ctx.cur(pfx + "_gof"))
            ctx.wait("tensor", pfx + "_yadd", max(0, ctx.cur(pfx + "_yadd") - 1))
            def y_mm(e, b=wb, t=tsl):
                r = None
                for oc in range(FC):
                    r = e.matmul(ps_y[b][:, :], wos[b][:, oc, :], ofb[:, oc, t],
                                 start=(oc == 0), stop=(oc == FC - 1))
                return r
            ctx.op("tensor", y_mm, sem=pfx + "_ymm")
            ctx.wait("vector", pfx + "_ymm", ctx.cur(pfx + "_ymm"))
            ctx.op("vector", lambda e, b=wb, f=fc, t=tsl:
                   e.tensor_add(resid[:, f, t], resid[:, f, t], ps_y[b][:, :]),
                   sem=pfx + "_yadd")

    ctx.wait("sync", pfx + "_yadd", ctx.cur(pfx + "_yadd"))
    ctx.op("sync", lambda e: e.nop(), sem="resid_ok")
    if "bres" in _DBG:
        t = dout("bres", (128, FC, TQ), F32)
        ctx.wait("sync", pfx + "_yadd", ctx.cur(pfx + "_yadd"))
        ctx.dma("sync", t.ap(), resid[:, :, :], "dbg")
    pps2.close()
    oes.close()
    fes.close()





def _assemble(res):
    out = np.empty((B, S, D), dtype=np.float32)
    for c in range(NCORE):
        b, q = c // 4, c % 4
        out[b, q * TQ:(q + 1) * TQ, :] = res.results[c]["yf"].T
    return out


_LAST_RESULTS = None


def kernel(**inputs):
    global _LAST_RESULTS
    in_maps = _host_prepare(inputs)
    nc = _build()
    res = run_bass_kernel_spmd(nc, in_maps, core_ids=list(range(NCORE)))
    _LAST_RESULTS = res
    return _assemble(res)
